# version 03
# baseline (speedup 1.0000x reference)
import numpy as np
import concourse.bass as bass
import concourse.mybir as mybir
from concourse.alu_op_type import AluOpType as ALU
from concourse.bass_utils import run_bass_kernel_spmd

F32 = mybir.dt.float32
F16 = mybir.dt.float16
I32 = mybir.dt.int32
AF = mybir.ActivationFunctionType
X = mybir.AxisListType.X

T = 2048
DM = 2048
INTER = 4096
NST = 16
RNK = 128
NT = T // 128
NCI = INTER // 128
LN2 = float(np.log(2.0))
QSCALE = float(448 * 0.8)
AR_COLS = 42000


class Builder:
    def __init__(self, nc):
        self.nc = nc
        self.ev = {k: [] for k in ("sync", "vector", "scalar", "tensor", "gpsimd")}
        self.sem = {}
        self.cnt = {}

    def new_sem(self, name):
        h = self.nc.alloc_semaphore(name)
        self.sem[name] = h
        self.cnt[name] = 0

    def op(self, eng, fn, sem=None, n=1):
        if sem is not None:
            self.cnt[sem] += n
            h = self.sem[sem]
            self.ev[eng].append(lambda e, fn=fn, h=h, n=n: fn(e).then_inc(h, n))
        else:
            self.ev[eng].append(lambda e, fn=fn: fn(e))

    def dma(self, q, out, in_, sem, accum=None, slow=False):
        self.cnt[sem] += 16
        h = self.sem[sem]
        if accum is None:
            self.ev[q].append(lambda e, o=out, i=in_, h=h, s=slow: e.dma_start(
                out=o, in_=i, allow_slow_non_contiguous=s).then_inc(h, 16))
        else:
            self.ev[q].append(lambda e, o=out, i=in_, h=h, a=accum, s=slow: e.dma_start(
                out=o, in_=i, accum_op=a, allow_slow_non_contiguous=s).then_inc(h, 16))

    def wait(self, eng, sem, val=None):
        v = self.cnt[sem] if val is None else val
        if v <= 0:
            return
        h = self.sem[sem]
        self.ev[eng].append(lambda e, h=h, v=v: e.wait_ge(h, v))

    def mark(self, sem):
        return self.cnt[sem]

    def stiny(self, fn):
        self.cnt["SS"] += 1
        h = self.sem["SS"]
        v = self.cnt["SS"]
        self.ev["scalar"].append(lambda e, fn=fn, h=h: fn(e).then_inc(h, 1))
        self.ev["scalar"].append(lambda e, h=h, v=v: e.wait_ge(h, v))

    def vtiny(self, fn):
        self.cnt["VS"] += 1
        h = self.sem["VS"]
        v = self.cnt["VS"]
        self.ev["vector"].append(lambda e, fn=fn, h=h: fn(e).then_inc(h, 1))
        self.ev["vector"].append(lambda e, h=h, v=v: e.wait_ge(h, v))


def build_program():
    nc = bass.Bass("TRN2", target_bir_lowering=False, detect_race_conditions=False)
    B = Builder(nc)
    for s in ("LD", "LS", "LG", "VD", "SD", "TD", "GD", "ST", "VS", "SS"):
        B.new_sem(s)

    def din(name, shape, dt=F32):
        return nc.dram_tensor(name, list(shape), dt, kind="ExternalInput").ap()

    x_d = din("x", (T, DM))
    w1_d = din("w1", (DM, 2 * INTER))
    cw_d = din("cw", (INTER, 4))
    cb_d = din("cb", (1, INTER))
    xw_d = din("xw", (INTER, 160))
    dtw_d = din("dtw", (RNK, INTER))
    dtb_d = din("dtb", (1, INTER))
    alog_d = din("alog", (INTER, NST))
    dd_d = din("dd", (1, INTER))
    wo_d = din("wo", (INTER, DM))
    w2_d = din("w2", (DM, DM))
    fb_d = din("fb", (1, DM))
    oh_d = din("onehot", (1, 8))
    out_d = nc.dram_tensor("out", [T, DM], F32, kind="ExternalOutput").ap()

    u_sp = nc.dram_tensor("u_sp", [INTER, T], F16, kind="Internal").ap()
    sg_sp = nc.dram_tensor("sg_sp", [INTER, T], F16, kind="Internal").ap()
    ym_sp = nc.dram_tensor("ym_sp", [INTER, T], F16, kind="Internal").ap()
    bc_sp = nc.dram_tensor("bc_sp", [32, T], F16, kind="Internal").ap()
    xq_sp = nc.dram_tensor("xq_sp", [DM, T], F16, kind="Internal").ap()
    wq_sp = nc.dram_tensor("wq_sp", [DM, DM], F16, kind="Internal").ap()
    sc_sp = nc.dram_tensor("sc_sp", [1, 16], F32, kind="Internal").ap()
    cc_in = nc.dram_tensor("cc_in", [1, 8], F32, kind="Internal").ap()
    cc_out = nc.dram_tensor("cc_out", [1, 8], F32, kind="Internal", addr_space="Shared").ap()

    sb = lambda name, sh, dt=F32: nc.alloc_sbuf_tensor(name, list(sh), dt).ap()
    negm = sb("negm", (128, NT)); inv = sb("inv", (128, NT)); nbinv = sb("nbinv", (128, NT))
    inv2 = sb("inv2", (128, NT)); nb2 = sb("nb2", (128, NT))
    rmx = sb("rmx", (128, NT)); rmn = sb("rmn", (128, NT)); amx = sb("amx", (128, NT))
    tinya = sb("tinya", (128, 8)); tinyb = sb("tinyb", (128, 8))
    zpad = sb("zpad", (128, 128))
    A_sb = sb("A_sb", (128, NCI * NST))
    D_sb = sb("D_sb", (128, NCI))
    cw_sb = sb("cw_sb", (128, NCI * 4))
    cb_sb = sb("cb_sb", (128, NCI))
    dtb_sb = sb("dtb_sb", (128, NCI))
    dtw_h = sb("dtw_h", (128, INTER), F16)
    xw_h = sb("xw_h", (128, NCI * 160), F16)
    id_h = sb("id_h", (128, 128), F16)
    id_f = sb("id_f", (128, 128), F32)
    oh_sb = sb("oh_sb", (1, 8))
    ag_sb = sb("ag_sb", (1, 8))
    ccin_sb = sb("ccin_sb", (1, 8))
    scal = sb("scal", (1, 16))
    srep = sb("srep", (128, 8))
    constc = sb("constc", (128, 4))
    dtinT = sb("dtinT", (128, T), F16)
    bcst = sb("bcst", (32, T), F16)

    arena = sb("arena", (128, AR_COLS), F32)
    ar_off = [0]

    def carve(cols):
        o = ar_off[0]
        assert o + cols <= AR_COLS, f"arena overflow {o}+{cols}"
        ar_off[0] = o + cols
        return arena[:, o:o + cols]

    def carve16(cols16):
        return carve(cols16 // 2).bitcast(F16)

    def ar_reset(v=0):
        ar_off[0] = v

    ps = nc.alloc_psum_tensor("ps", [128, 4096], F32).ap()

    xnT = carve16(16 * T)
    AR_B = ar_off[0]
    stg_alog = carve(NCI * NST)
    stg_dtw = carve(INTER)
    stg_xw = carve(NCI * 160)

    def cvec(d):
        return d.rearrange("a (c p) -> a c p", p=128)[0].transpose([1, 0])
    B.dma("sync", stg_alog.rearrange("p (c n) -> p c n", n=NST),
          alog_d.rearrange("(c p) n -> c p n", p=128).transpose([1, 0, 2]), "LD")
    B.dma("sync", cw_sb.rearrange("p (c k) -> p c k", k=4),
          cw_d.rearrange("(c p) k -> c p k", p=128).transpose([1, 0, 2]), "LD")
    B.dma("sync", cb_sb[:], cvec(cb_d), "LD", slow=True)
    B.dma("sync", dtb_sb[:], cvec(dtb_d), "LD", slow=True)
    B.dma("sync", D_sb[:], cvec(dd_d), "LD", slow=True)
    B.dma("sync", oh_sb[:], oh_d, "LD")
    B.dma("sync", stg_dtw[:], dtw_d, "LD")
    B.dma("sync", stg_xw.rearrange("p (c f) -> p c f", f=160),
          xw_d.rearrange("(c p) f -> c p f", p=128).transpose([1, 0, 2]), "LD")
    ld_pro = B.mark("LD")
    B.wait("scalar", "LD", ld_pro)
    B.op("scalar", lambda e: e.activation(out=A_sb[:], in_=stg_alog[:], func=AF.Exp), sem="SD")
    B.wait("vector", "SD", B.mark("SD"))
    B.op("vector", lambda e: e.tensor_scalar(out=A_sb[:], in0=A_sb[:], scalar1=-1.0, scalar2=None, op0=ALU.mult), sem="VD")
    B.wait("vector", "LD", ld_pro)
    B.op("vector", lambda e: e.tensor_copy(dtw_h[:], stg_dtw[:]), sem="VD")
    B.op("vector", lambda e: e.tensor_copy(xw_h[:], stg_xw[:]), sem="VD")
    B.op("vector", lambda e: e.memset(id_h[:], 0.0), sem="VD")
    B.op("vector", lambda e: e.memset(id_f[:], 0.0), sem="VD")
    B.op("vector", lambda e: e.memset(constc[:, 0:1], 1e-5), sem="VD")
    B.op("vector", lambda e: e.memset(constc[:, 1:2], 3.0 * LN2), sem="VD")
    B.op("vector", lambda e: e.memset(constc[:, 2:3], -3.0 * LN2), sem="VD")
    B.wait("gpsimd", "VD", B.mark("VD"))
    B.op("gpsimd", lambda e: e.affine_select(out=id_h[:], in_=id_h[:], compare_op=ALU.not_equal,
                                             fill=1.0, base=0, pattern=[[-1, 128]], channel_multiplier=1))
    B.op("gpsimd", lambda e: e.affine_select(out=id_f[:], in_=id_f[:], compare_op=ALU.not_equal,
                                             fill=1.0, base=0, pattern=[[-1, 128]], channel_multiplier=1), sem="GD")
    gd_ident = B.mark("GD")

    xt = [carve(T), carve(T)]
    sq = carve(T)
    xnh = [carve16(T), carve16(T)]
    pT = ps[:, 0:1024].bitcast(F16)
    xnT4 = xnT.rearrange("p (d t c) -> p d t c", d=16, c=128)

    B.wait("tensor", "GD", gd_ident)
    vA_marks = []
    sA_marks = []
    tA_marks = []
    for i in range(NT):
        buf = i % 2
        if i >= 2:
            B.wait("sync", "VD", vA_marks[i - 2])
            B.wait("sync", "SD", sA_marks[i - 2])
        B.dma("sync", xt[buf][:], x_d[i * 128:(i + 1) * 128, :], "LD")
        B.wait("vector", "LD")
        B.vtiny(lambda e, b=buf: e.reduce_sum(tinya[:, 0:1], xt[b][:], axis=X))
        B.vtiny(lambda e, i=i: e.tensor_scalar(out=negm[:, i:i + 1], in0=tinya[:, 0:1],
                                               scalar1=-1.0 / DM, scalar2=None, op0=ALU.mult))
        B.op("vector", lambda e: e.tensor_copy(tinya[:, 6:7], negm[:, 0:1]), sem="VD")
        B.op("vector", lambda e, b=buf, i=i: e.tensor_reduce(rmx[:, i:i + 1], xt[b][:], axis=X, op=ALU.max), sem="VD")
        B.op("vector", lambda e, b=buf, i=i: e.tensor_reduce(rmn[:, i:i + 1], xt[b][:], axis=X, op=ALU.min), sem="VD")
        vA_marks.append(B.mark("VD"))
        B.wait("scalar", "VD", vA_marks[-1])
        B.op("scalar", lambda e, b=buf, i=i: e.activation(out=sq[:], in_=xt[b][:], func=AF.Square,
                                                          bias=negm[:, i:i + 1]), sem="SD")
        B.wait("vector", "SD", B.mark("SD"))
        B.op("vector", lambda e: e.reduce_sum(tinya[:, 1:2], sq[:], axis=X), sem="VD")
        B.wait("scalar", "VD", B.mark("VD"))
        B.op("scalar", lambda e: e.activation(out=tinya[:, 2:3], in_=tinya[:, 1:2], func=AF.Sqrt,
                                              scale=1.0 / DM, bias=constc[:, 0:1]), sem="SD")
        B.wait("vector", "SD", B.mark("SD"))
        B.vtiny(lambda e, i=i: e.reciprocal(inv[:, i:i + 1], tinya[:, 2:3]))
        B.vtiny(lambda e, i=i: e.tensor_tensor(out=nbinv[:, i:i + 1], in0=negm[:, i:i + 1],
                                               in1=inv[:, i:i + 1], op=ALU.mult))
        B.vtiny(lambda e, i=i: e.tensor_tensor(out=tinya[:, 3:4], in0=rmx[:, i:i + 1], in1=negm[:, i:i + 1], op=ALU.add))
        B.vtiny(lambda e, i=i: e.tensor_tensor(out=tinya[:, 3:4], in0=tinya[:, 3:4], in1=inv[:, i:i + 1], op=ALU.mult))
        B.vtiny(lambda e, i=i: e.tensor_tensor(out=tinya[:, 4:5], in0=rmn[:, i:i + 1], in1=negm[:, i:i + 1], op=ALU.add))
        B.vtiny(lambda e, i=i: e.tensor_tensor(out=tinya[:, 4:5], in0=tinya[:, 4:5], in1=inv[:, i:i + 1], op=ALU.mult))
        B.vtiny(lambda e, i=i: e.tensor_scalar(out=tinya[:, 4:5], in0=tinya[:, 4:5], scalar1=-1.0,
                                               scalar2=None, op0=ALU.mult))
        B.op("vector", lambda e, i=i: e.tensor_tensor(out=amx[:, i:i + 1], in0=tinya[:, 3:4], in1=tinya[:, 4:5], op=ALU.max), sem="VD")
        B.wait("scalar", "VD", B.mark("VD"))
        if i >= 2:
            B.wait("scalar", "TD", tA_marks[i - 2])
        B.op("scalar", lambda e, b=buf, i=i: e.activation(out=xnh[b][:], in_=xt[b][:], func=AF.Identity,
                                                          scale=inv[:, i:i + 1], bias=nbinv[:, i:i + 1]), sem="SD")
        sA_marks.append(B.mark("SD"))
        B.wait("tensor", "SD", sA_marks[-1])
        for j in range(16):
            B.op("tensor", lambda e, b=buf, j=j: e.transpose(pT[:, j * 128:(j + 1) * 128],
                                                             xnh[b][:, j * 128:(j + 1) * 128], id_h[:]),
                 sem="TD" if j == 15 else None)
        tA_marks.append(B.mark("TD"))
        B.wait("vector", "TD", tA_marks[-1])
        B.op("vector", lambda e, i=i: e.tensor_copy(xnT4[:, :, i, :], pT[:]), sem="VD")
        B.wait("tensor", "VD", B.mark("VD"))

    B.vtiny(lambda e: e.tensor_reduce(tinya[:, 5:6], amx[:], axis=X, op=ALU.max))
    B.op("vector", lambda e: e.memset(zpad[:], 0.0))
    B.op("vector", lambda e: e.tensor_copy(zpad[:, 0:1], tinya[:, 5:6]), sem="VD")
    B.wait("tensor", "VD", B.mark("VD"))
    B.op("tensor", lambda e: e.transpose(ps[:, 3 * 512:3 * 512 + 128], zpad[:], id_f[:]), sem="TD")
    B.wait("vector", "TD", B.mark("TD"))
    B.vtiny(lambda e: e.tensor_reduce(scal[:, 0:1], ps[0:1, 3 * 512:3 * 512 + 128], axis=X, op=ALU.max))
    B.op("vector", lambda e: e.tensor_scalar(out=ccin_sb[:], in0=oh_sb[:], scalar1=scal[0:1, 0:1],
                                             scalar2=None, op0=ALU.mult), sem="VD")
    B.wait("gpsimd", "VD", B.mark("VD"))
    B.dma("gpsimd", cc_in, ccin_sb[:], "LG")
    B.wait("gpsimd", "LG")
    B.op("gpsimd", lambda e: e.collective_compute(kind="AllReduce", op=ALU.add,
                                                  replica_groups=[list(range(8))],
                                                  ins=[cc_in], outs=[cc_out]), sem="GD")
    B.wait("gpsimd", "GD", B.mark("GD"))
    B.dma("gpsimd", ag_sb[:], cc_out, "LG")
    B.wait("vector", "LG")
    B.vtiny(lambda e: e.tensor_reduce(scal[:, 1:2], ag_sb[:], axis=X, op=ALU.max))
    B.vtiny(lambda e: e.tensor_scalar(out=scal[:, 2:3], in0=scal[:, 1:2], scalar1=1.0 / QSCALE,
                                      scalar2=None, op0=ALU.mult))
    B.op("vector", lambda e: e.reciprocal(scal[:, 3:4], scal[:, 2:3]), sem="VD")
    B.wait("gpsimd", "VD", B.mark("VD"))
    B.dma("gpsimd", sc_sp[:, 0:4], scal[:, 0:4], "LG")
    B.wait("gpsimd", "LG")
    B.dma("gpsimd", srep[:, 0:4], sc_sp[:, 0:4].partition_broadcast(128), "LG")
    lg_scal1 = B.mark("LG")
    B.wait("vector", "LG", lg_scal1)
    B.op("vector", lambda e: e.tensor_scalar(out=inv2[:], in0=inv[:], scalar1=srep[:, 3:4], scalar2=None, op0=ALU.mult))
    B.op("vector", lambda e: e.tensor_scalar(out=nb2[:], in0=nbinv[:], scalar1=srep[:, 3:4], scalar2=None, op0=ALU.mult), sem="VD")
    endA_V = B.mark("VD")
    endA_S = B.mark("SD")

    ar_reset(AR_B)
    w1e = carve16(16 * 1024)
    w1st = [carve(1024), carve(1024)]
    hpad = [carve16(2054), carve16(2054)]
    xc = [carve(T), carve(T)]
    uo = [carve16(T), carve16(T)]
    pB = [ps[:, 0:2048], ps[:, 2048:4096]]

    B.op("vector", lambda e: e.memset(hpad[0][:, 0:3], 0.0))
    B.op("vector", lambda e: e.memset(hpad[1][:, 0:3], 0.0), sem="VD")

    w1v_marks = []
    evac_marks = []
    spill_marks = []
    tmm_marks = []
    B.wait("sync", "VD", endA_V)
    B.wait("sync", "SD", endA_S)
    for e8 in range(8):
        for dj in range(16):
            bw = dj % 2
            k = e8 * 16 + dj
            if k >= 2:
                B.wait("sync", "VD", w1v_marks[k - 2])
            if e8 >= 1 and dj == 0:
                B.wait("sync", "TD", tmm_marks[-1])
                B.wait("vector", "TD", tmm_marks[-1])
            B.dma("sync", w1st[bw][:], w1_d[dj * 128:(dj + 1) * 128, e8 * 1024:(e8 + 1) * 1024], "LD")
            B.wait("vector", "LD")
            B.op("vector", lambda e, b=bw, dj=dj: e.tensor_copy(w1e[:, dj * 1024:(dj + 1) * 1024], w1st[b][:]), sem="VD")
            w1v_marks.append(B.mark("VD"))
        wconv = B.mark("VD")
        for eb in range(8):
            gi = e8 * 8 + eb
            pb = gi % 2
            B.wait("tensor", "VD", wconv)
            if gi == 0:
                B.wait("tensor", "SD", endA_S)
            if gi >= 2:
                B.wait("tensor", "SD", evac_marks[gi - 2])
                B.wait("tensor", "VD", evac_marks_v[gi - 2])
            for dj in range(16):
                for tc in range(4):
                    B.op("tensor", lambda e, p=pb, dj=dj, tc=tc, eb=eb: e.matmul(
                        pB[p][:, tc * 512:(tc + 1) * 512],
                        lhsT=w1e[:, dj * 1024 + eb * 128:dj * 1024 + (eb + 1) * 128],
                        rhs=xnT4[:, dj, :, :].rearrange("p t c -> p (t c)")[:, tc * 512:(tc + 1) * 512],
                        start=(dj == 0), stop=(dj == 15)),
                        sem="TD" if (dj == 15 and tc == 3) else None)
            tmm_marks.append(B.mark("TD"))
            if gi == 0:
                evac_marks_v = []
            if gi < 32:
                ci = gi
                B.wait("vector", "TD", tmm_marks[-1])
                if gi >= 2:
                    B.wait("vector", "SD", spill_dep[gi - 2])
                B.op("vector", lambda e, p=pb: e.tensor_copy(hpad[p][:, 3:3 + T], pB[p][:]), sem="VD")
                B.op("vector", lambda e, b=pb, ci=ci: e.tensor_scalar(
                    out=xc[b][:], in0=hpad[b][:, 0:T], scalar1=cw_sb[:, ci * 4:ci * 4 + 1],
                    scalar2=None, op0=ALU.mult))
                for kk in range(1, 4):
                    B.op("vector", lambda e, b=pb, ci=ci, kk=kk: e.scalar_tensor_tensor(
                        out=xc[b][:], in0=hpad[b][:, kk:kk + T], scalar=cw_sb[:, ci * 4 + kk:ci * 4 + kk + 1],
                        in1=xc[b][:], op0=ALU.mult, op1=ALU.add), sem="VD" if kk == 3 else None)
                evac_marks_v.append(B.mark("VD"))
                B.wait("scalar", "VD", evac_marks_v[-1])
                if gi >= 2:
                    B.wait("scalar", "LS", spill_marks[gi - 2])
                B.op("scalar", lambda e, b=pb, ci=ci: e.activation(
                    out=uo[b][:], in_=xc[b][:], func=AF.Silu, bias=cb_sb[:, ci:ci + 1]), sem="SD")
                evac_marks.append(B.mark("SD"))
                B.wait("scalar", "SD", evac_marks[-1])
                B.dma("scalar", u_sp[ci * 128:(ci + 1) * 128, :], uo[pb][:], "LS")
                spill_marks.append(B.mark("LS"))
            else:
                ci = gi - 32
                B.wait("scalar", "TD", tmm_marks[-1])
                if gi >= 2:
                    B.wait("scalar", "LS", spill_marks[gi - 2])
                B.op("scalar", lambda e, p=pb: e.activation(out=uo[p][:], in_=pB[p][:], func=AF.Silu), sem="SD")
                evac_marks.append(B.mark("SD"))
                evac_marks_v.append(B.mark("VD"))
                B.wait("scalar", "SD", evac_marks[-1])
                B.dma("scalar", sg_sp[ci * 128:(ci + 1) * 128, :], uo[pb][:], "LS")
                spill_marks.append(B.mark("LS"))
            if gi == 0:
                spill_dep = []
            spill_dep.append(B.mark("SD"))
    endB_T = B.mark("TD")
    endB_S = B.mark("SD")
    endB_V = B.mark("VD")
    endB_LS = B.mark("LS")

    ar_reset(0)
    qv = carve(T); qsgn = carve(T); qt = carve(T); qe = carve(T)
    qiq = carve(T); qqm = carve(T); qu2 = carve(T)
    ql = qu2
    qri = carve(T).bitcast(I32); qr0 = carve(T)
    qc = qt
    qres = carve16(T)
    qst = [carve(T), carve(T)]
    qxq = carve16(T)
    wq_od = carve16(16 * T)
    pQ = ps[:, 0:1024].bitcast(F16)
    pQf = ps[:, 2048:2048 + 512]

    B.wait("sync", "TD", endB_T)
    B.wait("scalar", "TD", endB_T)
    B.wait("vector", "TD", endB_T)
    B.wait("vector", "SD", endB_S)

    def emit_quant(out_ap=None):
        if out_ap is None:
            out_ap = qres[:]
        B.wait("scalar", "VD", B.mark("VD"))
        B.op("scalar", lambda e: e.activation(out=qsgn[:], in_=qv[:], func=AF.Sign), sem="SD")
        B.wait("vector", "SD", B.mark("SD"))
        B.op("vector", lambda e: e.tensor_tensor(out=qt[:], in0=qv[:], in1=qsgn[:], op=ALU.mult))
        B.op("vector", lambda e: e.tensor_scalar(out=qt[:], in0=qt[:], scalar1=2.0 ** -11, scalar2=None, op0=ALU.max), sem="VD")
        B.wait("scalar", "VD", B.mark("VD"))
        B.op("scalar", lambda e: e.activation(out=ql[:], in_=qt[:], func=AF.Ln), sem="SD")
        B.wait("vector", "SD", B.mark("SD"))
        B.op("vector", lambda e: e.tensor_scalar(out=qe[:], in0=ql[:], scalar1=1.0 / LN2, scalar2=None,
                                                 op0=ALU.mult))
        B.op("vector", lambda e: e.tensor_copy(qri[:], qe[:]))
        B.op("vector", lambda e: e.tensor_copy(qr0[:], qri[:]))
        B.op("vector", lambda e: e.tensor_tensor(out=qu2[:], in0=qe[:], in1=qr0[:], op=ALU.subtract))
        B.op("vector", lambda e: e.tensor_scalar(out=qu2[:], in0=qu2[:], scalar1=0.0, scalar2=None, op0=ALU.is_lt))
        B.op("vector", lambda e: e.tensor_tensor(out=qe[:], in0=qr0[:], in1=qu2[:], op=ALU.subtract))
        B.op("vector", lambda e: e.tensor_scalar(out=qe[:], in0=qe[:], scalar1=-6.0, scalar2=None, op0=ALU.max), sem="VD")
        B.wait("scalar", "VD", B.mark("VD"))
        B.op("scalar", lambda e: e.activation(out=qiq[:], in_=qe[:], func=AF.Exp, scale=-LN2, bias=constc[:, 1:2]))
        B.op("scalar", lambda e: e.activation(out=qqm[:], in_=qe[:], func=AF.Exp, scale=LN2, bias=constc[:, 2:3]), sem="SD")
        B.wait("vector", "SD", B.mark("SD"))
        B.op("vector", lambda e: e.tensor_tensor(out=qu2[:], in0=qt[:], in1=qiq[:], op=ALU.mult))
        B.op("vector", lambda e: e.tensor_copy(qri[:], qu2[:]))
        B.op("vector", lambda e: e.tensor_copy(qr0[:], qri[:]))
        B.op("vector", lambda e: e.tensor_tensor(out=qu2[:], in0=qu2[:], in1=qr0[:], op=ALU.subtract))
        B.op("vector", lambda e: e.tensor_scalar(out=qc[:], in0=qu2[:], scalar1=0.5, scalar2=None, op0=ALU.is_ge))
        B.op("vector", lambda e: e.tensor_tensor(out=qr0[:], in0=qr0[:], in1=qc[:], op=ALU.add))
        B.op("vector", lambda e: e.tensor_scalar(out=qc[:], in0=qu2[:], scalar1=-0.5, scalar2=None, op0=ALU.is_lt))
        B.op("vector", lambda e: e.tensor_tensor(out=qr0[:], in0=qr0[:], in1=qc[:], op=ALU.subtract))
        B.op("vector", lambda e: e.tensor_tensor(out=qr0[:], in0=qr0[:], in1=qqm[:], op=ALU.mult))
        B.op("vector", lambda e, o=out_ap: e.tensor_tensor(out=o, in0=qr0[:], in1=qsgn[:], op=ALU.mult), sem="VD")

    xqT4 = xq_sp.rearrange("(a p) t -> p a t", p=128)
    qsd_marks = []
    qtd_marks = []
    for i in range(NT):
        bq = i % 2
        if i >= 2:
            B.wait("sync", "SD", qsd_marks[i - 2])
        B.dma("sync", qst[bq][:], x_d[i * 128:(i + 1) * 128, :], "LD")
        B.wait("scalar", "LD")
        if i >= 1:
            B.wait("scalar", "VD", B.mark("VD"))
        B.op("scalar", lambda e, b=bq, i=i: e.activation(out=qv[:], in_=qst[b][:], func=AF.Identity,
                                                         scale=inv2[:, i:i + 1], bias=nb2[:, i:i + 1]), sem="SD")
        qsd_marks.append(B.mark("SD"))
        B.wait("vector", "SD", qsd_marks[-1])
        if i >= 1:
            B.wait("vector", "TD", qtd_marks[-1])
        emit_quant()
        B.wait("tensor", "VD", B.mark("VD"))
        if i >= 1:
            B.wait("tensor", "VD", qev_marks[-1])
        for j in range(16):
            B.op("tensor", lambda e, j=j: e.transpose(pQ[:, j * 128:(j + 1) * 128],
                                                      qres[:, j * 128:(j + 1) * 128], id_h[:]),
                 sem="TD" if j == 15 else None)
        qtd_marks.append(B.mark("TD"))
        B.wait("vector", "TD", qtd_marks[-1])
        if i >= 1:
            B.wait("vector", "LS", qls_marks[-1])
        B.op("vector", lambda e: e.tensor_copy(qxq[:], pQ[:]), sem="VD")
        qev = B.mark("VD")
        if i == 0:
            qev_marks = []; qls_marks = []
        qev_marks.append(qev)
        B.wait("scalar", "VD", qev)
        B.dma("scalar", xqT4[:, :, i * 128:(i + 1) * 128],
              qxq.rearrange("p (a c) -> p a c", a=16), "LS")
        qls_marks.append(B.mark("LS"))

    B.op("vector", lambda e: e.memset(tinyb[:, 0:1], 0.0), sem="VD")
    B.wait("sync", "SD", B.mark("SD"))
    B.wait("sync", "VD", B.mark("VD"))
    wv_marks = []
    for k in range(16):
        bq = k % 2
        if k >= 2:
            B.wait("sync", "VD", wv_marks[k - 2])
        B.dma("sync", qst[bq][:], w2_d[k * 128:(k + 1) * 128, :], "LD")
        B.wait("vector", "LD")
        B.vtiny(lambda e, b=bq: e.tensor_reduce(tinyb[:, 1:2], qst[b][:], axis=X,
                                                op=ALU.max, apply_absolute_value=True))
        B.op("vector", lambda e: e.tensor_tensor(out=tinyb[:, 0:1], in0=tinyb[:, 0:1], in1=tinyb[:, 1:2], op=ALU.max), sem="VD")
        wv_marks.append(B.mark("VD"))
    B.op("vector", lambda e: e.memset(zpad[:], 0.0))
    B.op("vector", lambda e: e.tensor_copy(zpad[:, 0:1], tinyb[:, 0:1]), sem="VD")
    B.wait("tensor", "VD", B.mark("VD"))
    B.op("tensor", lambda e: e.transpose(pQf[:, 0:128], zpad[:], id_f[:]), sem="TD")
    B.wait("vector", "TD", B.mark("TD"))
    B.vtiny(lambda e: e.tensor_reduce(scal[:, 4:5], pQf[0:1, 0:128], axis=X, op=ALU.max))
    B.vtiny(lambda e: e.tensor_tensor(out=scal[:, 5:6], in0=scal[:, 4:5], in1=scal[:, 1:2], op=ALU.max))
    B.vtiny(lambda e: e.tensor_scalar(out=scal[:, 5:6], in0=scal[:, 5:6], scalar1=1.0 / QSCALE, scalar2=None, op0=ALU.mult))
    B.vtiny(lambda e: e.reciprocal(scal[:, 6:7], scal[:, 5:6]))
    B.op("vector", lambda e: e.tensor_tensor(out=scal[:, 7:8], in0=scal[:, 2:3], in1=scal[:, 5:6], op=ALU.mult), sem="VD")
    B.wait("gpsimd", "VD", B.mark("VD"))
    B.dma("gpsimd", sc_sp[:, 4:8], scal[:, 4:8], "LG")
    B.wait("gpsimd", "LG")
    B.dma("gpsimd", srep[:, 4:8], sc_sp[:, 4:8].partition_broadcast(128), "LG")
    B.wait("vector", "LG", B.mark("LG"))
    wq_marks = []
    for k in range(16):
        bq = k % 2
        if k >= 2:
            B.wait("sync", "VD", wq_marks[k - 2])
        B.dma("sync", qst[bq][:], w2_d[k * 128:(k + 1) * 128, :], "LD")
        B.wait("vector", "LD")
        B.op("vector", lambda e, b=bq: e.tensor_scalar(out=qv[:], in0=qst[b][:],
                                                       scalar1=srep[:, 6:7], scalar2=None, op0=ALU.mult), sem="VD")
        wq_marks.append(B.mark("VD"))
        emit_quant(wq_od[:, k * T:(k + 1) * T])
    wq_all = B.mark("VD")
    B.wait("tensor", "VD", wq_all)
    wtr_marks = []
    for dk in range(16):
        if dk >= 1:
            B.wait("tensor", "VD", wtr_marks[-1])
        for oj in range(16):
            B.op("tensor", lambda e, dk=dk, oj=oj: e.transpose(
                pQ[:, oj * 128:(oj + 1) * 128],
                wq_od[:, oj * T + dk * 128:oj * T + (dk + 1) * 128], id_h[:]),
                sem="TD" if oj == 15 else None)
        B.wait("vector", "TD", B.mark("TD"))
        if dk >= 1:
            B.wait("vector", "LS", B.mark("LS"))
        B.op("vector", lambda e: e.tensor_copy(qxq[:], pQ[:]), sem="VD")
        wtr_marks.append(B.mark("VD"))
        B.wait("scalar", "VD", wtr_marks[-1])
        B.dma("scalar", wq_sp[dk * 128:(dk + 1) * 128, :], qxq[:], "LS")
    endQ_LS = B.mark("LS")

    ar_reset(0)
    uc = [carve16(T), carve16(T)]
    pC0 = ps[:, 0:2048]
    pC1 = ps[0:32, 2048:4096]
    B.wait("scalar", "LS", endQ_LS)
    c_marks = []
    for ci in range(NCI):
        bc = ci % 2
        if ci >= 2:
            B.wait("scalar", "TD", c_marks[ci - 2])
        B.dma("scalar", uc[bc][:], u_sp[ci * 128:(ci + 1) * 128, :], "LS")
        B.wait("tensor", "LS")
        for tc in range(4):
            B.op("tensor", lambda e, b=bc, ci=ci, tc=tc: e.matmul(
                pC0[:, tc * 512:(tc + 1) * 512],
                lhsT=xw_h[:, ci * 160:ci * 160 + 128],
                rhs=uc[b][:, tc * 512:(tc + 1) * 512],
                start=(ci == 0), stop=(ci == NCI - 1)))
        for tc in range(4):
            B.op("tensor", lambda e, b=bc, ci=ci, tc=tc: e.matmul(
                pC1[:, tc * 512:(tc + 1) * 512],
                lhsT=xw_h[:, ci * 160 + 128:ci * 160 + 160],
                rhs=uc[b][:, tc * 512:(tc + 1) * 512],
                start=(ci == 0), stop=(ci == NCI - 1)),
                sem="TD" if tc == 3 else None)
        c_marks.append(B.mark("TD"))
    B.wait("scalar", "TD", B.mark("TD"))
    B.op("scalar", lambda e: e.activation(out=dtinT[:], in_=pC0[:], func=AF.Identity), sem="SD")
    dtin_ready = B.mark("SD")
    B.wait("vector", "TD", B.mark("TD"))
    B.op("vector", lambda e: e.tensor_copy(bcst[:], pC1[:]), sem="VD")
    B.wait("gpsimd", "VD", B.mark("VD"))
    B.dma("gpsimd", bc_sp[:, :], bcst[:], "LG")
    lg_bc = B.mark("LG")

    ar_reset(0)
    CPE = 4
    dt_e = [carve16(T) for _ in range(CPE)]
    u_e = [carve16(T) for _ in range(CPE)]
    w_e = [carve16(T) for _ in range(CPE)]
    y_acc = [carve(T) for _ in range(CPE)]
    ezp = [carve(T), carve(T)]
    ez_s = ezp[0]
    brep = [carve16(T), carve16(T)]
    crep = [carve16(T), carve16(T)]
    a_t = [carve(T), carve(T)]
    b_t = [carve16(T), carve16(T)]
    h_t = [carve16(T), carve16(T)]
    z_t = b_t
    sgc = [carve16(T), carve16(T)]
    ymo = [carve16(T), carve16(T)]
    pD = ps[:, 0:2048]

    B.wait("sync", "LG", lg_bc)
    scan_marks = {}
    nuse_marks = {}
    ymo_marks = {}
    ue_marks = {}
    scan_it = 0
    bcast_it = 0
    for e8 in range(8):
        for c4 in range(CPE):
            ci = e8 * CPE + c4
            B.wait("tensor", "SD", B.mark("SD"))
            if e8 == 0 and c4 == 0:
                B.wait("tensor", "SD", dtin_ready)
            for tc in range(4):
                B.op("tensor", lambda e, ci=ci, tc=tc: e.matmul(
                    pD[:, tc * 512:(tc + 1) * 512],
                    lhsT=dtw_h[:, ci * 128:(ci + 1) * 128],
                    rhs=dtinT[:, tc * 512:(tc + 1) * 512],
                    start=True, stop=True), sem="TD" if tc == 3 else None)
            B.wait("scalar", "TD", B.mark("TD"))
            if e8 >= 1:
                B.wait("scalar", "VD", ue_marks[(e8 - 1, "v")])
            B.op("scalar", lambda e, ci=ci: e.activation(out=ez_s[:], in_=pD[:], func=AF.Exp,
                                                         bias=dtb_sb[:, ci:ci + 1]))
            B.op("scalar", lambda e, c4=c4: e.activation(out=dt_e[c4][:], in_=ez_s[:], func=AF.Ln, bias=1.0), sem="SD")
        sd_dt = B.mark("SD")
        for c4 in range(CPE):
            ci = e8 * CPE + c4
            if e8 >= 1:
                B.wait("scalar", "SD", ue_marks[(e8 - 1, "s")])
            B.dma("scalar", u_e[c4][:], u_sp[ci * 128:(ci + 1) * 128, :], "LS")
        B.wait("vector", "LS")
        B.wait("vector", "SD", sd_dt)
        for c4 in range(CPE):
            B.op("vector", lambda e, c4=c4: e.tensor_tensor(out=w_e[c4][:], in0=dt_e[c4][:], in1=u_e[c4][:], op=ALU.mult))
            B.op("vector", lambda e, c4=c4: e.memset(y_acc[c4][:], 0.0), sem="VD")
        for n in range(NST):
            nb = n % 2
            if bcast_it >= 2:
                B.wait("sync", "VD", nuse_marks[nb])
            B.dma("sync", brep[nb][:], bc_sp[n:n + 1, :].partition_broadcast(128), "LD")
            B.dma("sync", crep[nb][:], bc_sp[16 + n:17 + n, :].partition_broadcast(128), "LD")
            bcast_it += 1
            B.wait("vector", "LD")
            B.wait("scalar", "LD")
            for c4 in range(CPE):
                ci = e8 * CPE + c4
                ab = scan_it % 2
                if scan_it >= 2:
                    B.wait("scalar", "VD", scan_marks[ab])
                B.op("scalar", lambda e, c4=c4, ci=ci, n=n, ab=ab: e.activation(
                    out=a_t[ab][:], in_=dt_e[c4][:], func=AF.Exp,
                    scale=A_sb[:, ci * NST + n:ci * NST + n + 1]), sem="SD")
                B.wait("vector", "SD", B.mark("SD"))
                if scan_it >= 2:
                    B.wait("vector", "GD", acc_marks[scan_it - 2])
                B.op("vector", lambda e, c4=c4, nb=nb, ab=ab: e.tensor_tensor(
                    out=b_t[ab][:], in0=w_e[c4][:], in1=brep[nb][:], op=ALU.mult))
                B.op("vector", lambda e, ab=ab: e.tensor_tensor_scan(
                    out=h_t[ab][:], data0=a_t[ab][:], data1=b_t[ab][:],
                    initial=0.0, op0=ALU.mult, op1=ALU.add))
                B.op("vector", lambda e, ab=ab, nb=nb: e.tensor_tensor(
                    out=z_t[ab][:], in0=h_t[ab][:], in1=crep[nb][:], op=ALU.mult), sem="VD")
                scan_marks[ab] = B.mark("VD")
                B.wait("gpsimd", "VD", scan_marks[ab])
                B.op("gpsimd", lambda e, ab=ab, c4=c4: e.tensor_tensor(
                    out=y_acc[c4][:], in0=y_acc[c4][:], in1=z_t[ab][:], op=ALU.add), sem="GD")
                if scan_it == 0:
                    acc_marks = {}
                    acc_last = {}
                acc_marks[scan_it] = B.mark("GD")
                acc_last[c4] = B.mark("GD")
                scan_it += 1
            nuse_marks[nb] = B.mark("VD")
        for c4 in range(CPE):
            ci = e8 * CPE + c4
            cb2 = c4 % 2
            if (e8, c4) not in [(0, 0), (0, 1)]:
                B.wait("scalar", "VD", sg_used.get(cb2, 0))
            B.dma("scalar", sgc[cb2][:], sg_sp[ci * 128:(ci + 1) * 128, :], "LS")
            B.wait("scalar", "LS")
            B.op("scalar", lambda e, c4=c4, ci=ci, cb2=cb2: e.activation(
                out=ezp[cb2][:], in_=u_e[c4][:], func=AF.Identity, scale=D_sb[:, ci:ci + 1]), sem="SD")
            B.wait("vector", "SD", B.mark("SD"))
            B.wait("vector", "LS", B.mark("LS"))
            B.wait("vector", "GD", acc_last[c4])
            if cb2 in ymo_marks:
                B.wait("vector", "LG", ymo_marks[cb2])
            B.op("vector", lambda e, c4=c4, cb2=cb2: e.tensor_tensor(out=y_acc[c4][:], in0=y_acc[c4][:], in1=ezp[cb2][:], op=ALU.add))
            B.op("vector", lambda e, c4=c4, cb2=cb2: e.tensor_tensor(out=ymo[cb2][:], in0=y_acc[c4][:],
                                                                     in1=sgc[cb2][:], op=ALU.mult), sem="VD")
            if e8 == 0 and c4 == 0:
                sg_used = {}
            sg_used[cb2] = B.mark("VD")
            B.wait("gpsimd", "VD", B.mark("VD"))
            B.dma("gpsimd", ym_sp[ci * 128:(ci + 1) * 128, :], ymo[cb2][:], "LG")
            ymo_marks[cb2] = B.mark("LG")
        ue_marks[(e8, "v")] = B.mark("VD")
        ue_marks[(e8, "s")] = B.mark("SD")
    endD_LG = B.mark("LG")

    ar_reset(0)
    woh = carve16(32 * 1024)
    wqh = carve16(16 * 1024)
    wst = [carve(1024), carve(1024)]
    ymp = [carve16(32 * 128), carve16(32 * 128)]
    xqp = [carve16(16 * 128), carve16(16 * 128)]
    xres = [carve(1024), carve(1024)]
    gg = [carve(1024), carve(1024)]
    oo = [carve(1024), carve(1024)]
    fbias = carve(DM)
    pS_ = [ps[:, 0:1024], ps[:, 1024:2048]]
    pF_ = [ps[:, 2048:3072], ps[:, 3072:4096]]

    B.wait("gpsimd", "LG", endD_LG)
    B.wait("sync", "LG", endD_LG)
    B.wait("scalar", "LG", endD_LG)
    B.dma("sync", fbias[:], fb_d.partition_broadcast(128), "LD")

    ym4 = ym_sp.rearrange("(a p) t -> p a t", p=128)
    xq4 = xq_sp.rearrange("(a p) t -> p a t", p=128)
    wq4 = wq_sp.rearrange("(a p) o -> p a o", p=128)

    ew_marks = {}
    ev_marks = {}
    es_marks = {}
    est_marks = {}
    epan_marks = {}
    tile_it = 0
    for oh in range(2):
        for ib in range(32):
            bw = ib % 2
            k = oh * 32 + ib
            if k >= 2:
                B.wait("sync", "VD", ew_marks[k - 2])
            if ib == 0 and oh >= 1:
                B.wait("sync", "TD", B.mark("TD"))
                B.wait("vector", "TD", B.mark("TD"))
            B.dma("sync", wst[bw][:], wo_d[ib * 128:(ib + 1) * 128, oh * 1024:(oh + 1) * 1024], "LD")
            B.wait("vector", "LD")
            B.op("vector", lambda e, b=bw, ib=ib: e.tensor_copy(woh[:, ib * 1024:(ib + 1) * 1024], wst[b][:]), sem="VD")
            ew_marks[k] = B.mark("VD")
        B.dma("sync", wqh.rearrange("p (a c) -> p a c", a=16), wq4[:, :, oh * 1024:(oh + 1) * 1024], "LD")
        wq_ld = B.mark("LD")
        B.wait("tensor", "LD", wq_ld)
        B.wait("tensor", "VD", B.mark("VD"))
        for tb in range(NT):
            pb = tb % 2
            if tile_it >= 2:
                B.wait("scalar", "TD", epan_marks[pb])
            B.dma("scalar", ymp[pb][:].rearrange("p (a c) -> p a c", a=32),
                  ym4[:, :, tb * 128:(tb + 1) * 128], "LS")
            B.dma("scalar", xqp[pb][:].rearrange("p (a c) -> p a c", a=16),
                  xq4[:, :, tb * 128:(tb + 1) * 128], "LS")
            if tile_it >= 2:
                B.wait("sync", "VD", ev_marks[pb])
            B.dma("sync", xres[pb][:], x_d[tb * 128:(tb + 1) * 128, oh * 1024:(oh + 1) * 1024], "LD")
            B.wait("tensor", "LS")
            B.wait("tensor", "LD")
            if tile_it >= 2:
                B.wait("tensor", "VD", ev_marks[pb])
                B.wait("tensor", "SD", es_marks[pb])
            for sub in range(2):
                for ib in range(32):
                    B.op("tensor", lambda e, pb=pb, sub=sub, ib=ib: e.matmul(
                        pS_[pb][:, sub * 512:(sub + 1) * 512],
                        lhsT=ymp[pb][:, ib * 128:(ib + 1) * 128],
                        rhs=woh[:, ib * 1024 + sub * 512:ib * 1024 + (sub + 1) * 512],
                        start=(ib == 0), stop=(ib == 31)))
                for dj in range(16):
                    B.op("tensor", lambda e, pb=pb, sub=sub, dj=dj: e.matmul(
                        pF_[pb][:, sub * 512:(sub + 1) * 512],
                        lhsT=xqp[pb][:, dj * 128:(dj + 1) * 128],
                        rhs=wqh[:, dj * 1024 + sub * 512:dj * 1024 + (sub + 1) * 512],
                        start=(dj == 0), stop=(dj == 15)),
                        sem="TD" if (sub == 1 and dj == 15) else None)
            epan_marks[pb] = B.mark("TD")
            B.wait("scalar", "TD", epan_marks[pb])
            B.op("scalar", lambda e, pb=pb: e.activation(out=gg[pb][:], in_=pF_[pb][:], func=AF.Identity,
                                                         scale=srep[:, 7:8]), sem="SD")
            B.wait("vector", "SD", B.mark("SD"))
            B.op("vector", lambda e, pb=pb, oh=oh: e.tensor_tensor(
                out=gg[pb][:], in0=gg[pb][:], in1=fbias[:, oh * 1024:(oh + 1) * 1024], op=ALU.add), sem="VD")
            B.wait("scalar", "VD", B.mark("VD"))
            B.op("scalar", lambda e, pb=pb: e.activation(out=gg[pb][:], in_=gg[pb][:], func=AF.Silu), sem="SD")
            es_marks[pb] = B.mark("SD")
            B.wait("vector", "SD", es_marks[pb])
            if tile_it >= 2:
                B.wait("vector", "ST", est_marks[pb])
            B.op("vector", lambda e, pb=pb: e.tensor_tensor(out=oo[pb][:], in0=pS_[pb][:], in1=gg[pb][:], op=ALU.mult))
            B.op("vector", lambda e, pb=pb: e.tensor_tensor(out=oo[pb][:], in0=oo[pb][:], in1=xres[pb][:], op=ALU.add), sem="VD")
            ev_marks[pb] = B.mark("VD")
            B.wait("sync", "VD", ev_marks[pb])
            B.dma("sync", out_d[tb * 128:(tb + 1) * 128, oh * 1024:(oh + 1) * 1024], oo[pb][:], "ST")
            est_marks[pb] = B.mark("ST")
            tile_it += 1

    B.wait("sync", "ST", B.mark("ST"))

    with nc.Block() as block:
        @block.sync
        def _(sync):
            for f in B.ev["sync"]:
                f(sync)

        @block.vector
        def _(vector):
            for f in B.ev["vector"]:
                f(vector)

        @block.scalar
        def _(scalar):
            for f in B.ev["scalar"]:
                f(scalar)

        @block.tensor
        def _(tensor):
            for f in B.ev["tensor"]:
                f(tensor)

        @block.gpsimd
        def _(gpsimd):
            for f in B.ev["gpsimd"]:
                f(gpsimd)

    return nc


_NC_CACHE = {}


def kernel(L0_attn_input_states, L0_attn_in_proj_w, L0_attn_conv_w, L0_attn_conv_b,
           L0_attn_x_proj_w, L0_attn_dt_proj_w, L0_attn_dt_proj_b, L0_attn_A_log,
           L0_attn_D, L0_attn_out_proj_w, L0_ffn_weight, L0_ffn_bias):
    x = np.ascontiguousarray(np.asarray(L0_attn_input_states, dtype=np.float32))
    if "nc" not in _NC_CACHE:
        _NC_CACHE["nc"] = build_program()
    nc = _NC_CACHE["nc"]
    common = {
        "w1": np.ascontiguousarray(L0_attn_in_proj_w, dtype=np.float32),
        "cw": np.ascontiguousarray(np.asarray(L0_attn_conv_w, dtype=np.float32).reshape(INTER, 4)),
        "cb": np.ascontiguousarray(np.asarray(L0_attn_conv_b, dtype=np.float32).reshape(1, INTER)),
        "xw": np.ascontiguousarray(L0_attn_x_proj_w, dtype=np.float32),
        "dtw": np.ascontiguousarray(L0_attn_dt_proj_w, dtype=np.float32),
        "dtb": np.ascontiguousarray(np.asarray(L0_attn_dt_proj_b, dtype=np.float32).reshape(1, INTER)),
        "alog": np.ascontiguousarray(L0_attn_A_log, dtype=np.float32),
        "dd": np.ascontiguousarray(np.asarray(L0_attn_D, dtype=np.float32).reshape(1, INTER)),
        "wo": np.ascontiguousarray(L0_attn_out_proj_w, dtype=np.float32),
        "w2": np.ascontiguousarray(L0_ffn_weight, dtype=np.float32),
        "fb": np.ascontiguousarray(np.asarray(L0_ffn_bias, dtype=np.float32).reshape(1, DM)),
    }
    in_maps = []
    for c in range(8):
        onehot = np.zeros((1, 8), dtype=np.float32)
        onehot[0, c] = 1.0
        m = dict(common)
        m["x"] = np.ascontiguousarray(x[c])
        m["onehot"] = onehot
        in_maps.append(m)
    res = run_bass_kernel_spmd(nc, in_maps, core_ids=list(range(8)))
    return np.stack([np.asarray(r["out"], dtype=np.float32) for r in res.results], axis=0)


# version 05
# speedup vs baseline: 1.2331x; 1.2331x over previous
import numpy as np
import concourse.bass as bass
import concourse.mybir as mybir
from concourse.alu_op_type import AluOpType as ALU
from concourse.bass_utils import run_bass_kernel_spmd

F32 = mybir.dt.float32
F16 = mybir.dt.float16
I32 = mybir.dt.int32
AF = mybir.ActivationFunctionType
X = mybir.AxisListType.X

T = 2048
DM = 2048
INTER = 4096
NST = 16
RNK = 128
NT = T // 128
NCI = INTER // 128
LN2 = float(np.log(2.0))
QSCALE = float(448 * 0.8)
AR_COLS = 42000


class Builder:
    def __init__(self, nc):
        self.nc = nc
        self.ev = {k: [] for k in ("sync", "vector", "scalar", "tensor", "gpsimd")}
        self.sem = {}
        self.cnt = {}

    def new_sem(self, name):
        h = self.nc.alloc_semaphore(name)
        self.sem[name] = h
        self.cnt[name] = 0

    def op(self, eng, fn, sem=None, n=1):
        if sem is not None:
            self.cnt[sem] += n
            h = self.sem[sem]
            self.ev[eng].append(lambda e, fn=fn, h=h, n=n: fn(e).then_inc(h, n))
        else:
            self.ev[eng].append(lambda e, fn=fn: fn(e))

    def dma(self, q, out, in_, sem, accum=None, slow=False):
        self.cnt[sem] += 16
        h = self.sem[sem]
        if accum is None:
            self.ev[q].append(lambda e, o=out, i=in_, h=h, s=slow: e.dma_start(
                out=o, in_=i, allow_slow_non_contiguous=s).then_inc(h, 16))
        else:
            self.ev[q].append(lambda e, o=out, i=in_, h=h, a=accum, s=slow: e.dma_start(
                out=o, in_=i, accum_op=a, allow_slow_non_contiguous=s).then_inc(h, 16))

    def wait(self, eng, sem, val=None):
        v = self.cnt[sem] if val is None else val
        if v <= 0:
            return
        h = self.sem[sem]
        self.ev[eng].append(lambda e, h=h, v=v: e.wait_ge(h, v))

    def mark(self, sem):
        return self.cnt[sem]

    def stiny(self, fn):
        self.cnt["SS"] += 1
        h = self.sem["SS"]
        v = self.cnt["SS"]
        self.ev["scalar"].append(lambda e, fn=fn, h=h: fn(e).then_inc(h, 1))
        self.ev["scalar"].append(lambda e, h=h, v=v: e.wait_ge(h, v))

    def vtiny(self, fn):
        self.cnt["VS"] += 1
        h = self.sem["VS"]
        v = self.cnt["VS"]
        self.ev["vector"].append(lambda e, fn=fn, h=h: fn(e).then_inc(h, 1))
        self.ev["vector"].append(lambda e, h=h, v=v: e.wait_ge(h, v))


def build_program():
    nc = bass.Bass("TRN2", target_bir_lowering=False, detect_race_conditions=False)
    B = Builder(nc)
    for s in ("LD", "LS", "LG", "VD", "SD", "TD", "GD", "ST", "VS", "SS"):
        B.new_sem(s)

    def din(name, shape, dt=F32):
        return nc.dram_tensor(name, list(shape), dt, kind="ExternalInput").ap()

    x_d = din("x", (T, DM))
    w1_d = din("w1", (DM, 2 * INTER))
    cw_d = din("cw", (INTER, 4))
    cb_d = din("cb", (1, INTER))
    xw_d = din("xw", (INTER, 160))
    dtw_d = din("dtw", (RNK, INTER))
    dtb_d = din("dtb", (1, INTER))
    alog_d = din("alog", (INTER, NST))
    dd_d = din("dd", (1, INTER))
    wo_d = din("wo", (INTER, DM))
    w2_d = din("w2", (DM, DM))
    fb_d = din("fb", (1, DM))
    oh_d = din("onehot", (1, 8))
    out_d = nc.dram_tensor("out", [T, DM], F32, kind="ExternalOutput").ap()

    u_sp = nc.dram_tensor("u_sp", [INTER, T], F16, kind="Internal").ap()
    sg_sp = nc.dram_tensor("sg_sp", [INTER, T], F16, kind="Internal").ap()
    ym_sp = nc.dram_tensor("ym_sp", [INTER, T], F16, kind="Internal").ap()
    bc_sp = nc.dram_tensor("bc_sp", [32, T], F16, kind="Internal").ap()
    xq_sp = nc.dram_tensor("xq_sp", [DM, T], F16, kind="Internal").ap()
    wq_sp = nc.dram_tensor("wq_sp", [DM, DM], F16, kind="Internal").ap()
    sc_sp = nc.dram_tensor("sc_sp", [1, 16], F32, kind="Internal").ap()
    cc_in = nc.dram_tensor("cc_in", [1, 8], F32, kind="Internal").ap()
    cc_out = nc.dram_tensor("cc_out", [1, 8], F32, kind="Internal", addr_space="Shared").ap()

    sb = lambda name, sh, dt=F32: nc.alloc_sbuf_tensor(name, list(sh), dt).ap()
    negm = sb("negm", (128, NT)); inv = sb("inv", (128, NT)); nbinv = sb("nbinv", (128, NT))
    inv2 = sb("inv2", (128, NT)); nb2 = sb("nb2", (128, NT))
    rmx = sb("rmx", (128, NT)); rmn = sb("rmn", (128, NT)); amx = sb("amx", (128, NT))
    tinya = sb("tinya", (128, 8)); tinyb = sb("tinyb", (128, 8))
    zpad = sb("zpad", (128, 128))
    A_sb = sb("A_sb", (128, NCI * NST))
    D_sb = sb("D_sb", (128, NCI))
    cw_sb = sb("cw_sb", (128, NCI * 4))
    cb_sb = sb("cb_sb", (128, NCI))
    dtb_sb = sb("dtb_sb", (128, NCI))
    dtw_h = sb("dtw_h", (128, INTER), F16)
    xw_h = sb("xw_h", (128, NCI * 160), F16)
    id_h = sb("id_h", (128, 128), F16)
    id_f = sb("id_f", (128, 128), F32)
    oh_sb = sb("oh_sb", (1, 8))
    ag_sb = sb("ag_sb", (1, 8))
    ccin_sb = sb("ccin_sb", (1, 8))
    scal = sb("scal", (1, 16))
    srep = sb("srep", (128, 8))
    constc = sb("constc", (128, 4))
    dtinT = sb("dtinT", (128, T), F16)
    bcst = sb("bcst", (32, T), F16)

    arena = sb("arena", (128, AR_COLS), F32)
    ar_off = [0]

    def carve(cols):
        o = ar_off[0]
        assert o + cols <= AR_COLS, f"arena overflow {o}+{cols}"
        ar_off[0] = o + cols
        return arena[:, o:o + cols]

    def carve16(cols16):
        return carve(cols16 // 2).bitcast(F16)

    def ar_reset(v=0):
        ar_off[0] = v

    ps = nc.alloc_psum_tensor("ps", [128, 4096], F32).ap()

    xnT = carve16(16 * T)
    AR_B = ar_off[0]
    stg_alog = carve(NCI * NST)
    stg_dtw = carve(INTER)
    stg_xw = carve(NCI * 160)

    def cvec(d):
        return d.rearrange("a (c p) -> a c p", p=128)[0].transpose([1, 0])
    B.dma("sync", stg_alog.rearrange("p (c n) -> p c n", n=NST),
          alog_d.rearrange("(c p) n -> c p n", p=128).transpose([1, 0, 2]), "LD")
    B.dma("sync", cw_sb.rearrange("p (c k) -> p c k", k=4),
          cw_d.rearrange("(c p) k -> c p k", p=128).transpose([1, 0, 2]), "LD")
    B.dma("sync", cb_sb[:], cvec(cb_d), "LD", slow=True)
    B.dma("sync", dtb_sb[:], cvec(dtb_d), "LD", slow=True)
    B.dma("sync", D_sb[:], cvec(dd_d), "LD", slow=True)
    B.dma("sync", oh_sb[:], oh_d, "LD")
    B.dma("sync", stg_dtw[:], dtw_d, "LD")
    B.dma("sync", stg_xw.rearrange("p (c f) -> p c f", f=160),
          xw_d.rearrange("(c p) f -> c p f", p=128).transpose([1, 0, 2]), "LD")
    ld_pro = B.mark("LD")
    B.wait("scalar", "LD", ld_pro)
    B.op("scalar", lambda e: e.activation(out=A_sb[:], in_=stg_alog[:], func=AF.Exp), sem="SD")
    B.wait("vector", "SD", B.mark("SD"))
    B.op("vector", lambda e: e.tensor_scalar(out=A_sb[:], in0=A_sb[:], scalar1=-1.0, scalar2=None, op0=ALU.mult), sem="VD")
    B.wait("vector", "LD", ld_pro)
    B.op("vector", lambda e: e.tensor_copy(dtw_h[:], stg_dtw[:]), sem="VD")
    B.op("vector", lambda e: e.tensor_copy(xw_h[:], stg_xw[:]), sem="VD")
    B.op("vector", lambda e: e.memset(id_h[:], 0.0), sem="VD")
    B.op("vector", lambda e: e.memset(id_f[:], 0.0), sem="VD")
    B.op("vector", lambda e: e.memset(constc[:, 0:1], 1e-5), sem="VD")
    B.op("vector", lambda e: e.memset(constc[:, 1:2], 3.0 * LN2), sem="VD")
    B.op("vector", lambda e: e.memset(constc[:, 2:3], -3.0 * LN2), sem="VD")
    B.wait("gpsimd", "VD", B.mark("VD"))
    B.op("gpsimd", lambda e: e.affine_select(out=id_h[:], in_=id_h[:], compare_op=ALU.not_equal,
                                             fill=1.0, base=0, pattern=[[-1, 128]], channel_multiplier=1))
    B.op("gpsimd", lambda e: e.affine_select(out=id_f[:], in_=id_f[:], compare_op=ALU.not_equal,
                                             fill=1.0, base=0, pattern=[[-1, 128]], channel_multiplier=1), sem="GD")
    gd_ident = B.mark("GD")

    xt = [carve(T), carve(T)]
    sq = carve(T)
    xnh = [carve16(T), carve16(T)]
    pT = ps[:, 0:1024].bitcast(F16)
    xnT4 = xnT.rearrange("p (d t c) -> p d t c", d=16, c=128)

    B.wait("tensor", "GD", gd_ident)
    vA_marks = []
    sA_marks = []
    tA_marks = []
    for i in range(NT):
        buf = i % 2
        if i >= 2:
            B.wait("sync", "VD", vA_marks[i - 2])
            B.wait("sync", "SD", sA_marks[i - 2])
        B.dma("sync", xt[buf][:], x_d[i * 128:(i + 1) * 128, :], "LD")
        B.wait("vector", "LD")
        B.vtiny(lambda e, b=buf: e.reduce_sum(tinya[:, 0:1], xt[b][:], axis=X))
        B.vtiny(lambda e, i=i: e.tensor_scalar(out=negm[:, i:i + 1], in0=tinya[:, 0:1],
                                               scalar1=-1.0 / DM, scalar2=None, op0=ALU.mult))
        B.op("vector", lambda e: e.tensor_copy(tinya[:, 6:7], negm[:, 0:1]), sem="VD")
        B.op("vector", lambda e, b=buf, i=i: e.tensor_reduce(rmx[:, i:i + 1], xt[b][:], axis=X, op=ALU.max), sem="VD")
        B.op("vector", lambda e, b=buf, i=i: e.tensor_reduce(rmn[:, i:i + 1], xt[b][:], axis=X, op=ALU.min), sem="VD")
        vA_marks.append(B.mark("VD"))
        B.wait("scalar", "VD", vA_marks[-1])
        B.op("scalar", lambda e, b=buf, i=i: e.activation(out=sq[:], in_=xt[b][:], func=AF.Square,
                                                          bias=negm[:, i:i + 1]), sem="SD")
        B.wait("vector", "SD", B.mark("SD"))
        B.op("vector", lambda e: e.reduce_sum(tinya[:, 1:2], sq[:], axis=X), sem="VD")
        B.wait("scalar", "VD", B.mark("VD"))
        B.op("scalar", lambda e: e.activation(out=tinya[:, 2:3], in_=tinya[:, 1:2], func=AF.Sqrt,
                                              scale=1.0 / DM, bias=constc[:, 0:1]), sem="SD")
        B.wait("vector", "SD", B.mark("SD"))
        B.vtiny(lambda e, i=i: e.reciprocal(inv[:, i:i + 1], tinya[:, 2:3]))
        B.vtiny(lambda e, i=i: e.tensor_tensor(out=nbinv[:, i:i + 1], in0=negm[:, i:i + 1],
                                               in1=inv[:, i:i + 1], op=ALU.mult))
        B.vtiny(lambda e, i=i: e.tensor_tensor(out=tinya[:, 3:4], in0=rmx[:, i:i + 1], in1=negm[:, i:i + 1], op=ALU.add))
        B.vtiny(lambda e, i=i: e.tensor_tensor(out=tinya[:, 3:4], in0=tinya[:, 3:4], in1=inv[:, i:i + 1], op=ALU.mult))
        B.vtiny(lambda e, i=i: e.tensor_tensor(out=tinya[:, 4:5], in0=rmn[:, i:i + 1], in1=negm[:, i:i + 1], op=ALU.add))
        B.vtiny(lambda e, i=i: e.tensor_tensor(out=tinya[:, 4:5], in0=tinya[:, 4:5], in1=inv[:, i:i + 1], op=ALU.mult))
        B.vtiny(lambda e, i=i: e.tensor_scalar(out=tinya[:, 4:5], in0=tinya[:, 4:5], scalar1=-1.0,
                                               scalar2=None, op0=ALU.mult))
        B.op("vector", lambda e, i=i: e.tensor_tensor(out=amx[:, i:i + 1], in0=tinya[:, 3:4], in1=tinya[:, 4:5], op=ALU.max), sem="VD")
        B.wait("scalar", "VD", B.mark("VD"))
        if i >= 2:
            B.wait("scalar", "TD", tA_marks[i - 2])
        B.op("scalar", lambda e, b=buf, i=i: e.activation(out=xnh[b][:], in_=xt[b][:], func=AF.Identity,
                                                          scale=inv[:, i:i + 1], bias=nbinv[:, i:i + 1]), sem="SD")
        sA_marks.append(B.mark("SD"))
        B.wait("tensor", "SD", sA_marks[-1])
        for j in range(16):
            B.op("tensor", lambda e, b=buf, j=j: e.transpose(pT[:, j * 128:(j + 1) * 128],
                                                             xnh[b][:, j * 128:(j + 1) * 128], id_h[:]),
                 sem="TD" if j == 15 else None)
        tA_marks.append(B.mark("TD"))
        B.wait("vector", "TD", tA_marks[-1])
        B.op("vector", lambda e, i=i: e.tensor_copy(xnT4[:, :, i, :], pT[:]), sem="VD")
        B.wait("tensor", "VD", B.mark("VD"))

    B.vtiny(lambda e: e.tensor_reduce(tinya[:, 5:6], amx[:], axis=X, op=ALU.max))
    B.op("vector", lambda e: e.memset(zpad[:], 0.0))
    B.op("vector", lambda e: e.tensor_copy(zpad[:, 0:1], tinya[:, 5:6]), sem="VD")
    B.wait("tensor", "VD", B.mark("VD"))
    B.op("tensor", lambda e: e.transpose(ps[:, 3 * 512:3 * 512 + 128], zpad[:], id_f[:]), sem="TD")
    B.wait("vector", "TD", B.mark("TD"))
    B.vtiny(lambda e: e.tensor_reduce(scal[:, 0:1], ps[0:1, 3 * 512:3 * 512 + 128], axis=X, op=ALU.max))
    B.op("vector", lambda e: e.tensor_scalar(out=ccin_sb[:], in0=oh_sb[:], scalar1=scal[0:1, 0:1],
                                             scalar2=None, op0=ALU.mult), sem="VD")
    B.wait("gpsimd", "VD", B.mark("VD"))
    B.dma("gpsimd", cc_in, ccin_sb[:], "LG")
    B.wait("gpsimd", "LG")
    B.op("gpsimd", lambda e: e.collective_compute(kind="AllReduce", op=ALU.add,
                                                  replica_groups=[list(range(8))],
                                                  ins=[cc_in], outs=[cc_out]), sem="GD")
    B.wait("gpsimd", "GD", B.mark("GD"))
    B.dma("gpsimd", ag_sb[:], cc_out, "LG")
    B.wait("vector", "LG")
    B.vtiny(lambda e: e.tensor_reduce(scal[:, 1:2], ag_sb[:], axis=X, op=ALU.max))
    B.vtiny(lambda e: e.tensor_scalar(out=scal[:, 2:3], in0=scal[:, 1:2], scalar1=1.0 / QSCALE,
                                      scalar2=None, op0=ALU.mult))
    B.op("vector", lambda e: e.reciprocal(scal[:, 3:4], scal[:, 2:3]), sem="VD")
    B.wait("gpsimd", "VD", B.mark("VD"))
    B.dma("gpsimd", sc_sp[:, 0:4], scal[:, 0:4], "LG")
    B.wait("gpsimd", "LG")
    B.dma("gpsimd", srep[:, 0:4], sc_sp[:, 0:4].partition_broadcast(128), "LG")
    lg_scal1 = B.mark("LG")
    B.wait("vector", "LG", lg_scal1)
    B.op("vector", lambda e: e.tensor_scalar(out=inv2[:], in0=inv[:], scalar1=srep[:, 3:4], scalar2=None, op0=ALU.mult))
    B.op("vector", lambda e: e.tensor_scalar(out=nb2[:], in0=nbinv[:], scalar1=srep[:, 3:4], scalar2=None, op0=ALU.mult), sem="VD")
    endA_V = B.mark("VD")
    endA_S = B.mark("SD")

    ar_reset(AR_B)
    w1e = carve16(16 * 1024)
    w1st = [carve(1024), carve(1024)]
    hpad = [carve16(2054), carve16(2054)]
    xc = [carve(T), carve(T)]
    uo = [carve16(T), carve16(T)]
    pB = [ps[:, 0:2048], ps[:, 2048:4096]]

    B.op("vector", lambda e: e.memset(hpad[0][:, 0:3], 0.0))
    B.op("vector", lambda e: e.memset(hpad[1][:, 0:3], 0.0), sem="VD")

    w1v_marks = []
    evac_marks = []
    spill_marks = []
    tmm_marks = []
    B.wait("sync", "VD", endA_V)
    B.wait("sync", "SD", endA_S)
    for e8 in range(8):
        for dj in range(16):
            bw = dj % 2
            k = e8 * 16 + dj
            if k >= 2:
                B.wait("sync", "VD", w1v_marks[k - 2])
            if e8 >= 1 and dj == 0:
                B.wait("sync", "TD", tmm_marks[-1])
                B.wait("vector", "TD", tmm_marks[-1])
            B.dma("sync", w1st[bw][:], w1_d[dj * 128:(dj + 1) * 128, e8 * 1024:(e8 + 1) * 1024], "LD")
            B.wait("vector", "LD")
            B.op("vector", lambda e, b=bw, dj=dj: e.tensor_copy(w1e[:, dj * 1024:(dj + 1) * 1024], w1st[b][:]), sem="VD")
            w1v_marks.append(B.mark("VD"))
        wconv = B.mark("VD")
        for eb in range(8):
            gi = e8 * 8 + eb
            pb = gi % 2
            B.wait("tensor", "VD", wconv)
            if gi == 0:
                B.wait("tensor", "SD", endA_S)
            if gi >= 2:
                B.wait("tensor", "SD", evac_marks[gi - 2])
                B.wait("tensor", "VD", evac_marks_v[gi - 2])
            for dj in range(16):
                for tc in range(4):
                    B.op("tensor", lambda e, p=pb, dj=dj, tc=tc, eb=eb: e.matmul(
                        pB[p][:, tc * 512:(tc + 1) * 512],
                        lhsT=w1e[:, dj * 1024 + eb * 128:dj * 1024 + (eb + 1) * 128],
                        rhs=xnT4[:, dj, :, :].rearrange("p t c -> p (t c)")[:, tc * 512:(tc + 1) * 512],
                        start=(dj == 0), stop=(dj == 15)),
                        sem="TD" if (dj == 15 and tc == 3) else None)
            tmm_marks.append(B.mark("TD"))
            if gi == 0:
                evac_marks_v = []
            if gi < 32:
                ci = gi
                B.wait("vector", "TD", tmm_marks[-1])
                if gi >= 2:
                    B.wait("vector", "SD", spill_dep[gi - 2])
                B.op("vector", lambda e, p=pb: e.tensor_copy(hpad[p][:, 3:3 + T], pB[p][:]), sem="VD")
                B.op("vector", lambda e, b=pb, ci=ci: e.tensor_scalar(
                    out=xc[b][:], in0=hpad[b][:, 0:T], scalar1=cw_sb[:, ci * 4:ci * 4 + 1],
                    scalar2=None, op0=ALU.mult))
                for kk in range(1, 4):
                    B.op("vector", lambda e, b=pb, ci=ci, kk=kk: e.scalar_tensor_tensor(
                        out=xc[b][:], in0=hpad[b][:, kk:kk + T], scalar=cw_sb[:, ci * 4 + kk:ci * 4 + kk + 1],
                        in1=xc[b][:], op0=ALU.mult, op1=ALU.add), sem="VD" if kk == 3 else None)
                evac_marks_v.append(B.mark("VD"))
                B.wait("scalar", "VD", evac_marks_v[-1])
                if gi >= 2:
                    B.wait("scalar", "LS", spill_marks[gi - 2])
                B.op("scalar", lambda e, b=pb, ci=ci: e.activation(
                    out=uo[b][:], in_=xc[b][:], func=AF.Silu, bias=cb_sb[:, ci:ci + 1]), sem="SD")
                evac_marks.append(B.mark("SD"))
                B.wait("scalar", "SD", evac_marks[-1])
                B.dma("scalar", u_sp[ci * 128:(ci + 1) * 128, :], uo[pb][:], "LS")
                spill_marks.append(B.mark("LS"))
            else:
                ci = gi - 32
                B.wait("scalar", "TD", tmm_marks[-1])
                if gi >= 2:
                    B.wait("scalar", "LS", spill_marks[gi - 2])
                B.op("scalar", lambda e, p=pb: e.activation(out=uo[p][:], in_=pB[p][:], func=AF.Silu), sem="SD")
                evac_marks.append(B.mark("SD"))
                evac_marks_v.append(B.mark("VD"))
                B.wait("scalar", "SD", evac_marks[-1])
                B.dma("scalar", sg_sp[ci * 128:(ci + 1) * 128, :], uo[pb][:], "LS")
                spill_marks.append(B.mark("LS"))
            if gi == 0:
                spill_dep = []
            spill_dep.append(B.mark("SD"))
    endB_T = B.mark("TD")
    endB_S = B.mark("SD")
    endB_V = B.mark("VD")
    endB_LS = B.mark("LS")

    ar_reset(0)
    qv = carve(T); qsgn = carve(T); qt = carve(T); qe = carve(T)
    qiq = carve(T); qqm = carve(T); qu2 = carve(T)
    ql = qu2
    qri = carve(T).bitcast(I32); qr0 = carve(T)
    qc = qt
    qres = carve16(T)
    qst = [carve(T), carve(T)]
    qxq = carve16(T)
    wq_od = carve16(16 * T)
    pQ = ps[:, 0:1024].bitcast(F16)
    pQf = ps[:, 2048:2048 + 512]

    B.wait("sync", "TD", endB_T)
    B.wait("scalar", "TD", endB_T)
    B.wait("vector", "TD", endB_T)
    B.wait("vector", "SD", endB_S)

    def emit_quant(out_ap=None):
        if out_ap is None:
            out_ap = qres[:]
        B.wait("scalar", "VD", B.mark("VD"))
        B.op("scalar", lambda e: e.activation(out=qsgn[:], in_=qv[:], func=AF.Sign), sem="SD")
        B.wait("vector", "SD", B.mark("SD"))
        B.op("vector", lambda e: e.tensor_tensor(out=qt[:], in0=qv[:], in1=qsgn[:], op=ALU.mult))
        B.op("vector", lambda e: e.tensor_scalar(out=qt[:], in0=qt[:], scalar1=2.0 ** -11, scalar2=None, op0=ALU.max), sem="VD")
        B.wait("scalar", "VD", B.mark("VD"))
        B.op("scalar", lambda e: e.activation(out=ql[:], in_=qt[:], func=AF.Ln), sem="SD")
        B.wait("vector", "SD", B.mark("SD"))
        B.op("vector", lambda e: e.tensor_scalar(out=qe[:], in0=ql[:], scalar1=1.0 / LN2, scalar2=None,
                                                 op0=ALU.mult))
        B.op("vector", lambda e: e.tensor_copy(qri[:], qe[:]))
        B.op("vector", lambda e: e.tensor_copy(qr0[:], qri[:]))
        B.op("vector", lambda e: e.tensor_tensor(out=qu2[:], in0=qe[:], in1=qr0[:], op=ALU.subtract))
        B.op("vector", lambda e: e.tensor_scalar(out=qu2[:], in0=qu2[:], scalar1=0.0, scalar2=None, op0=ALU.is_lt))
        B.op("vector", lambda e: e.tensor_tensor(out=qe[:], in0=qr0[:], in1=qu2[:], op=ALU.subtract))
        B.op("vector", lambda e: e.tensor_scalar(out=qe[:], in0=qe[:], scalar1=-6.0, scalar2=None, op0=ALU.max), sem="VD")
        B.wait("scalar", "VD", B.mark("VD"))
        B.op("scalar", lambda e: e.activation(out=qiq[:], in_=qe[:], func=AF.Exp, scale=-LN2, bias=constc[:, 1:2]))
        B.op("scalar", lambda e: e.activation(out=qqm[:], in_=qe[:], func=AF.Exp, scale=LN2, bias=constc[:, 2:3]), sem="SD")
        B.wait("vector", "SD", B.mark("SD"))
        B.op("vector", lambda e: e.tensor_tensor(out=qu2[:], in0=qt[:], in1=qiq[:], op=ALU.mult))
        B.op("vector", lambda e: e.tensor_copy(qri[:], qu2[:]))
        B.op("vector", lambda e: e.tensor_copy(qr0[:], qri[:]))
        B.op("vector", lambda e: e.tensor_tensor(out=qu2[:], in0=qu2[:], in1=qr0[:], op=ALU.subtract))
        B.op("vector", lambda e: e.tensor_scalar(out=qc[:], in0=qu2[:], scalar1=0.5, scalar2=None, op0=ALU.is_ge))
        B.op("vector", lambda e: e.tensor_tensor(out=qr0[:], in0=qr0[:], in1=qc[:], op=ALU.add))
        B.op("vector", lambda e: e.tensor_scalar(out=qc[:], in0=qu2[:], scalar1=-0.5, scalar2=None, op0=ALU.is_lt))
        B.op("vector", lambda e: e.tensor_tensor(out=qr0[:], in0=qr0[:], in1=qc[:], op=ALU.subtract))
        B.op("vector", lambda e: e.tensor_tensor(out=qr0[:], in0=qr0[:], in1=qqm[:], op=ALU.mult))
        B.op("vector", lambda e, o=out_ap: e.tensor_tensor(out=o, in0=qr0[:], in1=qsgn[:], op=ALU.mult), sem="VD")

    xqT4 = xq_sp.rearrange("(a p) t -> p a t", p=128)
    qsd_marks = []
    qtd_marks = []
    for i in range(NT):
        bq = i % 2
        if i >= 2:
            B.wait("sync", "SD", qsd_marks[i - 2])
        B.dma("sync", qst[bq][:], x_d[i * 128:(i + 1) * 128, :], "LD")
        B.wait("scalar", "LD")
        if i >= 1:
            B.wait("scalar", "VD", B.mark("VD"))
        B.op("scalar", lambda e, b=bq, i=i: e.activation(out=qv[:], in_=qst[b][:], func=AF.Identity,
                                                         scale=inv2[:, i:i + 1], bias=nb2[:, i:i + 1]), sem="SD")
        qsd_marks.append(B.mark("SD"))
        B.wait("vector", "SD", qsd_marks[-1])
        if i >= 1:
            B.wait("vector", "TD", qtd_marks[-1])
        emit_quant()
        B.wait("tensor", "VD", B.mark("VD"))
        if i >= 1:
            B.wait("tensor", "VD", qev_marks[-1])
        for j in range(16):
            B.op("tensor", lambda e, j=j: e.transpose(pQ[:, j * 128:(j + 1) * 128],
                                                      qres[:, j * 128:(j + 1) * 128], id_h[:]),
                 sem="TD" if j == 15 else None)
        qtd_marks.append(B.mark("TD"))
        B.wait("vector", "TD", qtd_marks[-1])
        if i >= 1:
            B.wait("vector", "LS", qls_marks[-1])
        B.op("vector", lambda e: e.tensor_copy(qxq[:], pQ[:]), sem="VD")
        qev = B.mark("VD")
        if i == 0:
            qev_marks = []; qls_marks = []
        qev_marks.append(qev)
        B.wait("scalar", "VD", qev)
        B.dma("scalar", xqT4[:, :, i * 128:(i + 1) * 128],
              qxq.rearrange("p (a c) -> p a c", a=16), "LS")
        qls_marks.append(B.mark("LS"))

    B.op("vector", lambda e: e.memset(tinyb[:, 0:1], 0.0), sem="VD")
    B.wait("sync", "SD", B.mark("SD"))
    B.wait("sync", "VD", B.mark("VD"))
    wv_marks = []
    for k in range(16):
        bq = k % 2
        if k >= 2:
            B.wait("sync", "VD", wv_marks[k - 2])
        B.dma("sync", qst[bq][:], w2_d[k * 128:(k + 1) * 128, :], "LD")
        B.wait("vector", "LD")
        B.vtiny(lambda e, b=bq: e.tensor_reduce(tinyb[:, 1:2], qst[b][:], axis=X,
                                                op=ALU.max, apply_absolute_value=True))
        B.op("vector", lambda e: e.tensor_tensor(out=tinyb[:, 0:1], in0=tinyb[:, 0:1], in1=tinyb[:, 1:2], op=ALU.max), sem="VD")
        wv_marks.append(B.mark("VD"))
    B.op("vector", lambda e: e.memset(zpad[:], 0.0))
    B.op("vector", lambda e: e.tensor_copy(zpad[:, 0:1], tinyb[:, 0:1]), sem="VD")
    B.wait("tensor", "VD", B.mark("VD"))
    B.op("tensor", lambda e: e.transpose(pQf[:, 0:128], zpad[:], id_f[:]), sem="TD")
    B.wait("vector", "TD", B.mark("TD"))
    B.vtiny(lambda e: e.tensor_reduce(scal[:, 4:5], pQf[0:1, 0:128], axis=X, op=ALU.max))
    B.vtiny(lambda e: e.tensor_tensor(out=scal[:, 5:6], in0=scal[:, 4:5], in1=scal[:, 1:2], op=ALU.max))
    B.vtiny(lambda e: e.tensor_scalar(out=scal[:, 5:6], in0=scal[:, 5:6], scalar1=1.0 / QSCALE, scalar2=None, op0=ALU.mult))
    B.vtiny(lambda e: e.reciprocal(scal[:, 6:7], scal[:, 5:6]))
    B.op("vector", lambda e: e.tensor_tensor(out=scal[:, 7:8], in0=scal[:, 2:3], in1=scal[:, 5:6], op=ALU.mult), sem="VD")
    B.wait("gpsimd", "VD", B.mark("VD"))
    B.dma("gpsimd", sc_sp[:, 4:8], scal[:, 4:8], "LG")
    B.wait("gpsimd", "LG")
    B.dma("gpsimd", srep[:, 4:8], sc_sp[:, 4:8].partition_broadcast(128), "LG")
    B.wait("vector", "LG", B.mark("LG"))
    wq_marks = []
    for k in range(16):
        bq = k % 2
        if k >= 2:
            B.wait("sync", "VD", wq_marks[k - 2])
        B.dma("sync", qst[bq][:], w2_d[k * 128:(k + 1) * 128, :], "LD")
        B.wait("vector", "LD")
        B.op("vector", lambda e, b=bq: e.tensor_scalar(out=qv[:], in0=qst[b][:],
                                                       scalar1=srep[:, 6:7], scalar2=None, op0=ALU.mult), sem="VD")
        wq_marks.append(B.mark("VD"))
        emit_quant(wq_od[:, k * T:(k + 1) * T])
    wq_all = B.mark("VD")
    B.wait("tensor", "VD", wq_all)
    wtr_marks = []
    for dk in range(16):
        if dk >= 1:
            B.wait("tensor", "VD", wtr_marks[-1])
        for oj in range(16):
            B.op("tensor", lambda e, dk=dk, oj=oj: e.transpose(
                pQ[:, oj * 128:(oj + 1) * 128],
                wq_od[:, oj * T + dk * 128:oj * T + (dk + 1) * 128], id_h[:]),
                sem="TD" if oj == 15 else None)
        B.wait("vector", "TD", B.mark("TD"))
        if dk >= 1:
            B.wait("vector", "LS", B.mark("LS"))
        B.op("vector", lambda e: e.tensor_copy(qxq[:], pQ[:]), sem="VD")
        wtr_marks.append(B.mark("VD"))
        B.wait("scalar", "VD", wtr_marks[-1])
        B.dma("scalar", wq_sp[dk * 128:(dk + 1) * 128, :], qxq[:], "LS")
    endQ_LS = B.mark("LS")

    ar_reset(0)
    uc = [carve16(T), carve16(T)]
    pC0 = ps[:, 0:2048]
    pC1 = ps[0:32, 2048:4096]
    B.wait("scalar", "LS", endQ_LS)
    c_marks = []
    for ci in range(NCI):
        bc = ci % 2
        if ci >= 2:
            B.wait("scalar", "TD", c_marks[ci - 2])
        B.dma("scalar", uc[bc][:], u_sp[ci * 128:(ci + 1) * 128, :], "LS")
        B.wait("tensor", "LS")
        for tc in range(4):
            B.op("tensor", lambda e, b=bc, ci=ci, tc=tc: e.matmul(
                pC0[:, tc * 512:(tc + 1) * 512],
                lhsT=xw_h[:, ci * 160:ci * 160 + 128],
                rhs=uc[b][:, tc * 512:(tc + 1) * 512],
                start=(ci == 0), stop=(ci == NCI - 1)))
        for tc in range(4):
            B.op("tensor", lambda e, b=bc, ci=ci, tc=tc: e.matmul(
                pC1[:, tc * 512:(tc + 1) * 512],
                lhsT=xw_h[:, ci * 160 + 128:ci * 160 + 160],
                rhs=uc[b][:, tc * 512:(tc + 1) * 512],
                start=(ci == 0), stop=(ci == NCI - 1)),
                sem="TD" if tc == 3 else None)
        c_marks.append(B.mark("TD"))
    B.wait("scalar", "TD", B.mark("TD"))
    B.op("scalar", lambda e: e.activation(out=dtinT[:], in_=pC0[:], func=AF.Identity), sem="SD")
    dtin_ready = B.mark("SD")
    B.wait("vector", "TD", B.mark("TD"))
    B.op("vector", lambda e: e.tensor_copy(bcst[:], pC1[:]), sem="VD")
    pc_evac_v = B.mark("VD")
    B.wait("gpsimd", "VD", pc_evac_v)
    B.dma("gpsimd", bc_sp[:, :], bcst[:], "LG")
    lg_bc = B.mark("LG")

    ar_reset(0)
    CPE = 4
    dt_e = [carve16(T) for _ in range(CPE)]
    u_e = [carve16(T) for _ in range(CPE)]
    w_e = [carve16(T) for _ in range(CPE)]
    y_acc = [carve(T) for _ in range(CPE)]
    ezp = [carve(T), carve(T)]
    ez_s = ezp[0]
    brep = [carve16(T), carve16(T)]
    crep = [carve16(T), carve16(T)]
    a_t = [carve(T), carve(T)]
    b_t = [carve16(T), carve16(T)]
    h_t = [carve16(T), carve16(T)]
    z_t = b_t
    sgc = [carve16(T), carve16(T)]
    ymo = [carve16(T), carve16(T)]
    pD = ps[:, 0:2048]

    B.wait("sync", "LG", lg_bc)
    scan_marks = {}
    nuse_marks = {}
    ymo_marks = {}
    ue_marks = {}
    acc_marks = {}
    pev_marks = {0: pc_evac_v, 1: pc_evac_v}
    sg_used = {}
    scan_it = 0
    bcast_it = 0
    pY = [ps[:, 0:2048], ps[:, 2048:4096]]
    for e8 in range(8):
        for c4 in range(CPE):
            ci = e8 * CPE + c4
            B.wait("tensor", "SD", B.mark("SD"))
            if e8 == 0 and c4 == 0:
                B.wait("tensor", "SD", dtin_ready)
            if c4 == 0:
                B.wait("tensor", "VD", pev_marks[0])
            for tc in range(4):
                B.op("tensor", lambda e, ci=ci, tc=tc: e.matmul(
                    pD[:, tc * 512:(tc + 1) * 512],
                    lhsT=dtw_h[:, ci * 128:(ci + 1) * 128],
                    rhs=dtinT[:, tc * 512:(tc + 1) * 512],
                    start=True, stop=True), sem="TD" if tc == 3 else None)
            B.wait("scalar", "TD", B.mark("TD"))
            if e8 >= 1:
                B.wait("scalar", "VD", ue_marks[(e8 - 1, "v")])
            B.op("scalar", lambda e, ci=ci: e.activation(out=ez_s[:], in_=pD[:], func=AF.Exp,
                                                         bias=dtb_sb[:, ci:ci + 1]))
            B.op("scalar", lambda e, c4=c4: e.activation(out=dt_e[c4][:], in_=ez_s[:], func=AF.Ln, bias=1.0), sem="SD")
        sd_dt = B.mark("SD")
        for c4 in range(CPE):
            ci = e8 * CPE + c4
            if e8 >= 1:
                B.wait("scalar", "SD", ue_marks[(e8 - 1, "s")])
            B.dma("scalar", u_e[c4][:], u_sp[ci * 128:(ci + 1) * 128, :], "LS")
        B.wait("vector", "LS")
        B.wait("vector", "SD", sd_dt)
        for c4 in range(CPE):
            B.op("vector", lambda e, c4=c4: e.tensor_tensor(out=w_e[c4][:], in0=dt_e[c4][:], in1=u_e[c4][:], op=ALU.mult),
                 sem="VD" if c4 == CPE - 1 else None)
        for half in range(2):
            for n in range(NST):
                nb = n % 2
                if bcast_it >= 2:
                    B.wait("sync", "VD", nuse_marks[nb])
                B.dma("sync", brep[nb][:], bc_sp[n:n + 1, :].partition_broadcast(128), "LD")
                B.dma("sync", crep[nb][:], bc_sp[16 + n:17 + n, :].partition_broadcast(128), "LD")
                bcast_it += 1
                B.wait("vector", "LD")
                B.wait("scalar", "LD")
                for cp in range(2):
                    c4 = half * 2 + cp
                    ci = e8 * CPE + c4
                    ab = scan_it % 2
                    if scan_it >= 2:
                        B.wait("scalar", "VD", scan_marks[ab])
                    B.op("scalar", lambda e, c4=c4, ci=ci, n=n, ab=ab: e.activation(
                        out=a_t[ab][:], in_=dt_e[c4][:], func=AF.Exp,
                        scale=A_sb[:, ci * NST + n:ci * NST + n + 1]), sem="SD")
                    B.wait("vector", "SD", B.mark("SD"))
                    if scan_it >= 2:
                        B.wait("vector", "TD", acc_marks[scan_it - 2])
                    B.op("vector", lambda e, c4=c4, nb=nb, ab=ab: e.tensor_tensor(
                        out=b_t[ab][:], in0=w_e[c4][:], in1=brep[nb][:], op=ALU.mult))
                    B.op("vector", lambda e, ab=ab: e.tensor_tensor_scan(
                        out=h_t[ab][:], data0=a_t[ab][:], data1=b_t[ab][:],
                        initial=0.0, op0=ALU.mult, op1=ALU.add))
                    B.op("vector", lambda e, ab=ab, nb=nb: e.tensor_tensor(
                        out=z_t[ab][:], in0=h_t[ab][:], in1=crep[nb][:], op=ALU.mult), sem="VD")
                    scan_marks[ab] = B.mark("VD")
                    B.wait("tensor", "VD", scan_marks[ab])
                    if n == 0:
                        if half == 0 and cp == 0:
                            B.wait("tensor", "SD", sd_dt)
                        else:
                            B.wait("tensor", "VD", pev_marks[cp])
                    for q in range(4):
                        B.op("tensor", lambda e, cp=cp, ab=ab, n=n, q=q: e.matmul(
                            pY[cp][:, q * 512:(q + 1) * 512], lhsT=id_h[:],
                            rhs=z_t[ab][:, q * 512:(q + 1) * 512],
                            start=(n == 0), stop=(n == NST - 1)),
                            sem="TD" if q == 3 else None)
                    acc_marks[scan_it] = B.mark("TD")
                    scan_it += 1
                nuse_marks[nb] = B.mark("VD")
            for cp in range(2):
                c4 = half * 2 + cp
                ci = e8 * CPE + c4
                cb2 = c4 % 2
                if (e8, c4) not in [(0, 0), (0, 1)]:
                    B.wait("scalar", "VD", sg_used.get(cb2, 0))
                B.dma("scalar", sgc[cb2][:], sg_sp[ci * 128:(ci + 1) * 128, :], "LS")
                B.wait("scalar", "LS")
                B.op("scalar", lambda e, c4=c4, ci=ci, cb2=cb2: e.activation(
                    out=ezp[cb2][:], in_=u_e[c4][:], func=AF.Identity, scale=D_sb[:, ci:ci + 1]), sem="SD")
                B.wait("vector", "SD", B.mark("SD"))
                B.wait("vector", "LS", B.mark("LS"))
                B.wait("vector", "TD", acc_marks[scan_it - 2 + cp])
                if cb2 in ymo_marks:
                    B.wait("vector", "LG", ymo_marks[cb2])
                B.op("vector", lambda e, c4=c4, cp=cp, cb2=cb2: e.tensor_tensor(
                    out=y_acc[c4][:], in0=pY[cp][:], in1=ezp[cb2][:], op=ALU.add), sem="VD")
                pev_marks[cp] = B.mark("VD")
                B.op("vector", lambda e, c4=c4, cb2=cb2: e.tensor_tensor(out=ymo[cb2][:], in0=y_acc[c4][:],
                                                                         in1=sgc[cb2][:], op=ALU.mult), sem="VD")
                sg_used[cb2] = B.mark("VD")
                B.wait("gpsimd", "VD", B.mark("VD"))
                B.dma("gpsimd", ym_sp[ci * 128:(ci + 1) * 128, :], ymo[cb2][:], "LG")
                ymo_marks[cb2] = B.mark("LG")
        ue_marks[(e8, "v")] = B.mark("VD")
        ue_marks[(e8, "s")] = B.mark("SD")
    endD_LG = B.mark("LG")

    ar_reset(0)
    woh = carve16(32 * 1024)
    wqh = carve16(16 * 1024)
    wst = [carve(1024), carve(1024)]
    ymp = [carve16(32 * 128), carve16(32 * 128)]
    xqp = [carve16(16 * 128), carve16(16 * 128)]
    xres = [carve(1024), carve(1024)]
    gg = [carve(1024), carve(1024)]
    oo = [carve(1024), carve(1024)]
    fbias = carve(DM)
    pS_ = [ps[:, 0:1024], ps[:, 1024:2048]]
    pF_ = [ps[:, 2048:3072], ps[:, 3072:4096]]

    B.wait("gpsimd", "LG", endD_LG)
    B.wait("sync", "LG", endD_LG)
    B.wait("scalar", "LG", endD_LG)
    B.dma("sync", fbias[:], fb_d.partition_broadcast(128), "LD")

    ym4 = ym_sp.rearrange("(a p) t -> p a t", p=128)
    xq4 = xq_sp.rearrange("(a p) t -> p a t", p=128)
    wq4 = wq_sp.rearrange("(a p) o -> p a o", p=128)

    ew_marks = {}
    ev_marks = {}
    es_marks = {}
    est_marks = {}
    epan_marks = {}
    tile_it = 0
    for oh in range(2):
        for ib in range(32):
            bw = ib % 2
            k = oh * 32 + ib
            if k >= 2:
                B.wait("sync", "VD", ew_marks[k - 2])
            if ib == 0 and oh >= 1:
                B.wait("sync", "TD", B.mark("TD"))
                B.wait("vector", "TD", B.mark("TD"))
            B.dma("sync", wst[bw][:], wo_d[ib * 128:(ib + 1) * 128, oh * 1024:(oh + 1) * 1024], "LD")
            B.wait("vector", "LD")
            B.op("vector", lambda e, b=bw, ib=ib: e.tensor_copy(woh[:, ib * 1024:(ib + 1) * 1024], wst[b][:]), sem="VD")
            ew_marks[k] = B.mark("VD")
        B.dma("sync", wqh.rearrange("p (a c) -> p a c", a=16), wq4[:, :, oh * 1024:(oh + 1) * 1024], "LD")
        wq_ld = B.mark("LD")
        B.wait("tensor", "LD", wq_ld)
        B.wait("tensor", "VD", B.mark("VD"))
        for tb in range(NT):
            pb = tb % 2
            if tile_it >= 2:
                B.wait("scalar", "TD", epan_marks[pb])
            B.dma("scalar", ymp[pb][:].rearrange("p (a c) -> p a c", a=32),
                  ym4[:, :, tb * 128:(tb + 1) * 128], "LS")
            B.dma("scalar", xqp[pb][:].rearrange("p (a c) -> p a c", a=16),
                  xq4[:, :, tb * 128:(tb + 1) * 128], "LS")
            if tile_it >= 2:
                B.wait("sync", "VD", ev_marks[pb])
            B.dma("sync", xres[pb][:], x_d[tb * 128:(tb + 1) * 128, oh * 1024:(oh + 1) * 1024], "LD")
            B.wait("tensor", "LS")
            B.wait("tensor", "LD")
            if tile_it >= 2:
                B.wait("tensor", "VD", ev_marks[pb])
                B.wait("tensor", "SD", es_marks[pb])
            for sub in range(2):
                for ib in range(32):
                    B.op("tensor", lambda e, pb=pb, sub=sub, ib=ib: e.matmul(
                        pS_[pb][:, sub * 512:(sub + 1) * 512],
                        lhsT=ymp[pb][:, ib * 128:(ib + 1) * 128],
                        rhs=woh[:, ib * 1024 + sub * 512:ib * 1024 + (sub + 1) * 512],
                        start=(ib == 0), stop=(ib == 31)))
                for dj in range(16):
                    B.op("tensor", lambda e, pb=pb, sub=sub, dj=dj: e.matmul(
                        pF_[pb][:, sub * 512:(sub + 1) * 512],
                        lhsT=xqp[pb][:, dj * 128:(dj + 1) * 128],
                        rhs=wqh[:, dj * 1024 + sub * 512:dj * 1024 + (sub + 1) * 512],
                        start=(dj == 0), stop=(dj == 15)),
                        sem="TD" if (sub == 1 and dj == 15) else None)
            epan_marks[pb] = B.mark("TD")
            B.wait("scalar", "TD", epan_marks[pb])
            B.op("scalar", lambda e, pb=pb: e.activation(out=gg[pb][:], in_=pF_[pb][:], func=AF.Identity,
                                                         scale=srep[:, 7:8]), sem="SD")
            B.wait("vector", "SD", B.mark("SD"))
            B.op("vector", lambda e, pb=pb, oh=oh: e.tensor_tensor(
                out=gg[pb][:], in0=gg[pb][:], in1=fbias[:, oh * 1024:(oh + 1) * 1024], op=ALU.add), sem="VD")
            B.wait("scalar", "VD", B.mark("VD"))
            B.op("scalar", lambda e, pb=pb: e.activation(out=gg[pb][:], in_=gg[pb][:], func=AF.Silu), sem="SD")
            es_marks[pb] = B.mark("SD")
            B.wait("vector", "SD", es_marks[pb])
            if tile_it >= 2:
                B.wait("vector", "ST", est_marks[pb])
            B.op("vector", lambda e, pb=pb: e.tensor_tensor(out=oo[pb][:], in0=pS_[pb][:], in1=gg[pb][:], op=ALU.mult))
            B.op("vector", lambda e, pb=pb: e.tensor_tensor(out=oo[pb][:], in0=oo[pb][:], in1=xres[pb][:], op=ALU.add), sem="VD")
            ev_marks[pb] = B.mark("VD")
            B.wait("sync", "VD", ev_marks[pb])
            B.dma("sync", out_d[tb * 128:(tb + 1) * 128, oh * 1024:(oh + 1) * 1024], oo[pb][:], "ST")
            est_marks[pb] = B.mark("ST")
            tile_it += 1

    B.wait("sync", "ST", B.mark("ST"))

    with nc.Block() as block:
        @block.sync
        def _(sync):
            for f in B.ev["sync"]:
                f(sync)

        @block.vector
        def _(vector):
            for f in B.ev["vector"]:
                f(vector)

        @block.scalar
        def _(scalar):
            for f in B.ev["scalar"]:
                f(scalar)

        @block.tensor
        def _(tensor):
            for f in B.ev["tensor"]:
                f(tensor)

        @block.gpsimd
        def _(gpsimd):
            for f in B.ev["gpsimd"]:
                f(gpsimd)

    return nc


_NC_CACHE = {}


def kernel(L0_attn_input_states, L0_attn_in_proj_w, L0_attn_conv_w, L0_attn_conv_b,
           L0_attn_x_proj_w, L0_attn_dt_proj_w, L0_attn_dt_proj_b, L0_attn_A_log,
           L0_attn_D, L0_attn_out_proj_w, L0_ffn_weight, L0_ffn_bias):
    x = np.ascontiguousarray(np.asarray(L0_attn_input_states, dtype=np.float32))
    if "nc" not in _NC_CACHE:
        _NC_CACHE["nc"] = build_program()
    nc = _NC_CACHE["nc"]
    common = {
        "w1": np.ascontiguousarray(L0_attn_in_proj_w, dtype=np.float32),
        "cw": np.ascontiguousarray(np.asarray(L0_attn_conv_w, dtype=np.float32).reshape(INTER, 4)),
        "cb": np.ascontiguousarray(np.asarray(L0_attn_conv_b, dtype=np.float32).reshape(1, INTER)),
        "xw": np.ascontiguousarray(L0_attn_x_proj_w, dtype=np.float32),
        "dtw": np.ascontiguousarray(L0_attn_dt_proj_w, dtype=np.float32),
        "dtb": np.ascontiguousarray(np.asarray(L0_attn_dt_proj_b, dtype=np.float32).reshape(1, INTER)),
        "alog": np.ascontiguousarray(L0_attn_A_log, dtype=np.float32),
        "dd": np.ascontiguousarray(np.asarray(L0_attn_D, dtype=np.float32).reshape(1, INTER)),
        "wo": np.ascontiguousarray(L0_attn_out_proj_w, dtype=np.float32),
        "w2": np.ascontiguousarray(L0_ffn_weight, dtype=np.float32),
        "fb": np.ascontiguousarray(np.asarray(L0_ffn_bias, dtype=np.float32).reshape(1, DM)),
    }
    in_maps = []
    for c in range(8):
        onehot = np.zeros((1, 8), dtype=np.float32)
        onehot[0, c] = 1.0
        m = dict(common)
        m["x"] = np.ascontiguousarray(x[c])
        m["onehot"] = onehot
        in_maps.append(m)
    res = run_bass_kernel_spmd(nc, in_maps, core_ids=list(range(8)))
    return np.stack([np.asarray(r["out"], dtype=np.float32) for r in res.results], axis=0)
